# version 33
# baseline (speedup 1.0000x reference)
import contextlib
import numpy as np

import concourse.bass as bass
import concourse.mybir as mybir
from concourse.bass_utils import run_bass_kernel_spmd

F32 = mybir.dt.float32
BF16 = mybir.dt.bfloat16
AF = mybir.ActivationFunctionType
ALU = mybir.AluOpType

B, S, D = 2, 4096, 2048
NCORES = 8
T = (B * S) // NCORES
NT = D // 128
NSTATE = 16
NH = 8
NEXP, R = 8, 16
LNEPS = 1e-5

SAME_ENGINE_SYNC = True
DEBUG_SPILL = False



class Prog:
    DMAQ = {"sy": ("dsy", "dsy1", "dsy2"), "p": ("dgp", "dgp1", "dgp2"),
            "sc": ("dsc", "dsc1", "dsc2")}

    def __init__(self):
        self.ops = {e: [] for e in ("te", "v", "s", "p", "sy")}
        self.cnt = {}
        self.sat = {e: {} for e in self.ops}
        self.bufs = {}
        self.ccnames = []
        self.qrr = {}

    def buf(self, name, space, lo, hi):
        if name not in self.bufs:
            self.bufs[name] = dict(space=space, lo=lo, hi=hi, w=None, r=[])
        return name

    def _overlaps(self, name):
        b = self.bufs[name]
        return [n2 for n2, b2 in self.bufs.items()
                if b2["space"] == b["space"] and b2["lo"] < b["hi"] and b["lo"] < b2["hi"]]

    def _need(self, eng, ev, waits):
        if ev is None:
            return
        sem, val = ev
        if sem == eng:
            if not SAME_ENGINE_SYNC or val > self.cnt.get(eng, 0):
                return
        if self.sat[eng].get(sem, 0) >= val:
            return
        waits[sem] = max(waits.get(sem, 0), val)

    def _emit(self, eng, fn, reads, writes, inc_sem, inc_val, no_inc=False):
        waits = {}
        for b in reads:
            self._need(eng, self.bufs[b]["w"], waits)
        for b in writes:
            for ov in self._overlaps(b):
                self._need(eng, self.bufs[ov]["w"], waits)
                for ev in self.bufs[ov]["r"]:
                    self._need(eng, ev, waits)
        for sem, val in waits.items():
            self.sat[eng][sem] = max(self.sat[eng].get(sem, 0), val)
        if no_inc:
            ev = (inc_sem, self.cnt.get(inc_sem, 0) + inc_val)
            self.ops[eng].append((sorted(waits.items()), fn, (inc_sem, 0)))
            for b in writes:
                for ov in self._overlaps(b):
                    if ov != b:
                        self.bufs[ov]["w"] = None
                        self.bufs[ov]["r"] = []
                self.bufs[b]["w"] = ev
                self.bufs[b]["r"] = []
            for b in reads:
                self.bufs[b]["r"].append(ev)
            return ev
        self.cnt[inc_sem] = self.cnt.get(inc_sem, 0) + inc_val
        ev = (inc_sem, self.cnt[inc_sem])
        self.ops[eng].append((sorted(waits.items()), fn, (inc_sem, inc_val)))
        for b in writes:
            for ov in self._overlaps(b):
                if ov != b:
                    self.bufs[ov]["w"] = None
                    self.bufs[ov]["r"] = []
            self.bufs[b]["w"] = ev
            self.bufs[b]["r"] = []
        for b in reads:
            self.bufs[b]["r"].append(ev)
        return ev

    def op(self, eng, fn, reads=(), writes=(), inc=True):
        return self._emit(eng, fn, reads, writes, eng, 1, no_inc=not inc)

    ISSUER = {"sy": "sy", "p": "p", "sc": "s"}

    def dma(self, queue, fn, reads=(), writes=()):
        subs = self.DMAQ[queue]
        idx = self.qrr.get(queue, 0)
        self.qrr[queue] = (idx + 1) % len(subs)
        qsem = subs[idx]
        eng = self.ISSUER[queue]
        prev = self.cnt.get(qsem, 0)
        ev = self._emit(eng, fn, reads, writes, qsem, 16)
        if prev > 0:
            waits, fn2, inc = self.ops[eng][-1]
            if (qsem, prev) not in waits:
                waits = sorted(set(waits) | {(qsem, prev)})
                self.ops[eng][-1] = (waits, fn2, inc)
        return ev

    def cc(self, fn, reads=(), writes=()):
        name = f"cc{len(self.ccnames)}"
        self.ccnames.append(name)
        return self._emit("p", fn, reads, writes, name, 1)

    def wait_only(self, eng, sems):
        self.ops[eng].append((list(sems), lambda b: None, (None, 0)))

    def emit(self, nc, stack):
        sems = {}
        for s in sorted(set(self.cnt) | set(self.ccnames)):
            sems[s] = stack.enter_context(nc.semaphore(s))
        block = stack.enter_context(nc.Block())

        def section(builder, eng):
            for waits, fn, (isem, ival) in self.ops[eng]:
                for s, v in waits:
                    builder.wait_ge(sems[s], v)
                inst = fn(builder)
                if inst is not None and isem is not None and ival:
                    if isem.startswith("cc"):
                        inst.then_inc(sems[isem])
                    else:
                        inst.then_inc(sems[isem], ival)

        @block.tensor
        def _(b):
            section(b, "te")

        @block.vector
        def _(b):
            section(b, "v")

        @block.scalar
        def _(b):
            section(b, "s")

        @block.gpsimd
        def _(b):
            section(b, "p")

        @block.sync
        def _(b):
            section(b, "sy")


A0 = 0
B0 = 16384
C0 = 32768
SM = 40960

RING = [A0 + 1024 * r for r in range(12)]
E0P = (A0 + 12288, A0 + 13312)
OFF_DB = A0 + 14336

OFF_Y = C0
OFF_X1L = C0 + 1024
OFF_GATH = C0 + 2048
OFF_SUMM = C0 + 2176
OFF_TP = C0 + 2192
OFF_TN = C0 + 2200
OFF_HIN = C0 + 2208

OFF_RB = SM
OFF_RMB = SM + 1024
OFF_TMPB = SM + 2048
OFF_A2BF = SM + 3072
OFF_BW2BF = SM + 4096
OFF_RTRBF = SM + 5120
OFF_ASB = SM + 5184
OFF_CONVW = SM + 5440
OFF_CONVB = SM + 5504
OFF_DTPB = SM + 5520
OFF_DSKIP = SM + 5536
OFF_HALFB = SM + 5552
OFF_REP = SM + 5568
OFF_PREV = SM + 5696
OFF_CMB = SM + 5704
OFF_XBF = SM + 5712
OFF_XSQ = SM + 6224
OFF_LT = SM + 6736
OFF_MX = SM + 6768
OFF_MASKT = SM + 6772
OFF_HMBF = SM + 6804
OFF_T1A = SM + 7060
OFF_HALL = OFF_T1A
OFF_T1B = OFF_T1A
OFF_IDENT = SM + 7572
OFF_ONESBF = SM + 7700
OFF_CONSTS = SM + 7701
OFF_ONESC = SM + 7706
OFF_CM8 = SM + 7707
OFF_PREV8 = SM + 7771
OFF_TCH = SM + 7835
OFF_TSC = SM + 7899
OFF_PREV24 = SM + 7963
SM_END = SM + 7987
OFF_DELTA2 = (SM + 0, SM + 1024)
OFF_DU = SM + 2048
OFF_SDELTA = SM + 5712

OFF_XPJBF = C0
OFF_DTPJBF = C0 + 1280
OFF_DTBF = C0 + 2304
OFF_XN2 = C0 + 2816
OFF_UACC = C0 + 3840
OFF_TAILS = C0 + 4864
OFF_HX = C0 + 4912
OFF_HALO8 = C0 + 4920
OFF_WSTG = (C0 + 4096, C0 + 4608)
OFF_UT2 = (C0 + 2304, C0 + 3328)
OFF_WSTG2 = OFF_UT2
OFF_RT3 = C0 + 4944
OFF_MRT3 = C0 + 4947
TOTAL_WORDS = SM_END
assert TOTAL_WORDS <= 49100, TOTAL_WORDS


def build_program():
    nc = bass.Bass("TRN2", target_bir_lowering=False, debug=False)
    p = Prog()

    def din(name, shape, dtype=F32):
        ap = nc.dram_tensor(name, shape, dtype, kind="ExternalInput").ap()
        p.buf(name, "d:" + name, 0, 1)
        return ap

    x_h = din("x", [D, T])
    wb_h = din("wbase_t", [D, D])
    rtr_h = din("router_t", [128, 128])
    a2_h = din("a2_t", [128, 2048])
    bw2_h = din("bw2_t", [128, D])
    cw_h = din("convw", [128, 64])
    cb_h = din("convb", [128, 16])
    xp_h = din("xproj_t", [128, 2560])
    dtp_h = din("dtproj_t", [128, D])
    dtb_h = din("dtprojb", [128, 16])
    alog_h = din("alog", [128, 256])
    dsk_h = din("dskip", [128, 16])
    bb_h = din("bbase", [128, 16])
    rep_h = din("repmat", [NEXP, 128])
    prev_h = din("prevhot", [128, NCORES])
    cmb_h = din("cmb", [128, NCORES])
    cm8_h = din("cm8", [128, 8 * NCORES])
    prev8_h = din("prev8", [128, 8 * NCORES])
    prev24_h = din("prev24", [128, 24])
    id_h = din("ident", [128, 128])
    consts_h = din("consts", [128, 5])

    out_h = nc.dram_tensor("out", [D, T], F32, kind="ExternalOutput").ap()
    p.buf("out", "d:out", 0, 1)

    def dint(name, shape, dtype=F32, **kw):
        kind = "Internal"
        if DEBUG_SPILL and name in ("x1sp", "usp", "dsp", "ysp", "x2sp"):
            kind = "ExternalOutput"
        ap = nc.dram_tensor(name, shape, dtype, kind=kind, **kw).ap()
        p.buf(name, "d:" + name, 0, 1)
        return ap

    x1sp = dint("x1sp", [D, T])
    usp = dint("usp", [D, T])
    dsp = dint("dsp", [D, T])
    ysp = dint("ysp", [D, T])
    x2sp = dint("x2sp", [D, T])
    sdsp = dint("sdsp", [D, T])
    wbsp = dint("wbsp", [D, 2048], BF16)
    for _k in range(NT):
        for _nm in ("x1sp", "usp", "dsp", "ysp", "x2sp", "sdsp", "wbsp"):
            p.buf(f"{_nm}{_k}", "d:" + _nm, _k, _k + 1)
        p.bufs["x1sp"]["lo"], p.bufs["x1sp"]["hi"] = 0, NT
        p.bufs["usp"]["lo"], p.bufs["usp"]["hi"] = 0, NT
        p.bufs["dsp"]["lo"], p.bufs["dsp"]["hi"] = 0, NT
        p.bufs["ysp"]["lo"], p.bufs["ysp"]["hi"] = 0, NT
        p.bufs["x2sp"]["lo"], p.bufs["x2sp"]["hi"] = 0, NT
    bcB = dint("bcB", [NSTATE, T])
    bcC = dint("bcC", [NSTATE, T])
    stdr = dint("stdr", [2, T])
    stdt = dint("stdt", [2, 4])
    halo_in = dint("halo_in", [D, 3])
    halo_out = dint("halo_out", [NCORES * D, 3], addr_space="Shared")
    ccin = [dint(f"ccin{r}", [128, 2 * NH]) for r in range(32)]
    ccout = [dint(f"ccout{r}", [NCORES * 128, 2 * NH], addr_space="Shared")
             for r in range(32)]

    stack = contextlib.ExitStack()
    arena = stack.enter_context(nc.sbuf_tensor("arena", [128, TOTAL_WORDS], F32)).ap()
    pall = stack.enter_context(nc.psum_tensor("pall", [128, 4096], F32)).ap()
    row_m = stack.enter_context(nc.sbuf_tensor("row_m", [1, T], F32)).ap()
    row_e = stack.enter_context(nc.sbuf_tensor("row_e", [1, T], F32)).ap()
    row_l = stack.enter_context(nc.sbuf_tensor("row_l", [NEXP, 512], F32)).ap()
    row_msk = stack.enter_context(nc.sbuf_tensor("row_msk", [NEXP, 512], F32)).ap()
    row_bc = stack.enter_context(nc.sbuf_tensor("row_bc", [NSTATE, T], F32)).ap()
    for nm in ("row_m", "row_e", "row_l", "row_msk", "row_bc"):
        p.buf(nm, nm, 0, 1)

    def sb(off, words):
        return arena[:, off:off + words]

    def sbbf(off, fpwords):
        return arena[:, off:off + fpwords].bitcast(BF16)

    def reg(name, off, words):
        return p.buf(name, "sbuf", off, off + words)

    def regp(name, bank, nbanks=1):
        return p.buf(name, "psum", bank * 512, (bank + nbanks) * 512)

    def cst(i):
        return arena[:, OFF_CONSTS + i:OFF_CONSTS + i + 1]
    C_EPS, C_ONE, C_NEG1, C_NEGH, C_HALF = 0, 1, 2, 3, 4

    for k in range(NT):
        reg(f"xres{k}", A0 + 1024 * k, 1024)
    for qq in range(8):
        p.dma("sy" if qq % 2 == 0 else "sc", lambda b, qq=qq: b.dma_start(
            out=sb(A0 + 2048 * qq, 2048).rearrange("p (k t) -> p k t", k=2),
            in_=x_h.rearrange("(k p) t -> p k t", p=128)[:, 2 * qq:2 * (qq + 1), :]),
            reads=["x"], writes=[f"xres{2 * qq + k2}" for k2 in range(2)])

    reg("convw", OFF_CONVW, 64)
    p.dma("sy", lambda b: b.dma_start(out=sb(OFF_CONVW, 64), in_=cw_h[:]),
          reads=["convw"], writes=["convw"])
    reg("convb", OFF_CONVB, 16)
    p.dma("sy", lambda b: b.dma_start(out=sb(OFF_CONVB, 16), in_=cb_h[:]),
          reads=["convb"], writes=["convb"])
    reg("dtpb", OFF_DTPB, 16)
    p.dma("sy", lambda b: b.dma_start(out=sb(OFF_DTPB, 16), in_=dtb_h[:]),
          reads=["dtprojb"], writes=["dtpb"])
    reg("dskip", OFF_DSKIP, 16)
    p.dma("sy", lambda b: b.dma_start(out=sb(OFF_DSKIP, 16), in_=dsk_h[:]),
          reads=["dskip"], writes=["dskip"])
    reg("bbase", OFF_HALFB, 16)
    p.dma("sy", lambda b: b.dma_start(out=sb(OFF_HALFB, 16), in_=bb_h[:]),
          reads=["bbase"], writes=["bbase"])
    reg("tmpbc", OFF_TMPB, 1024)
    p.dma("sy", lambda b: b.dma_start(out=sb(OFF_TMPB, 256), in_=alog_h[:]),
          reads=["alog"], writes=["tmpbc"])
    reg("repm", OFF_REP, 128)
    p.dma("sy", lambda b: b.dma_start(
        out=arena[0:NEXP, OFF_REP:OFF_REP + 128], in_=rep_h[:]),
        reads=["repmat"], writes=["repm"])
    reg("prevb", OFF_PREV, 8)
    p.dma("sy", lambda b: b.dma_start(out=sb(OFF_PREV, 8), in_=prev_h[:]),
          reads=["prevhot"], writes=["prevb"])
    reg("cmbb", OFF_CMB, 8)
    p.dma("sy", lambda b: b.dma_start(out=sb(OFF_CMB, 8), in_=cmb_h[:]),
          reads=["cmb"], writes=["cmbb"])
    reg("cm8", OFF_CM8, 64)
    p.dma("sy", lambda b: b.dma_start(out=sb(OFF_CM8, 64), in_=cm8_h[:]),
          reads=["cm8"], writes=["cm8"])
    reg("prev8", OFF_PREV8, 64)
    p.dma("sy", lambda b: b.dma_start(out=sb(OFF_PREV8, 64), in_=prev8_h[:]),
          reads=["prev8"], writes=["prev8"])
    reg("prev24", OFF_PREV24, 24)
    p.dma("sy", lambda b: b.dma_start(out=sb(OFF_PREV24, 24), in_=prev24_h[:]),
          reads=["prev24"], writes=["prev24"])
    reg("ident", OFF_IDENT, 128)
    p.dma("sy", lambda b: b.dma_start(out=sb(OFF_IDENT, 128), in_=id_h[:]),
          reads=["ident"], writes=["ident"])
    reg("consts", OFF_CONSTS, 5)
    p.dma("sy", lambda b: b.dma_start(out=sb(OFF_CONSTS, 5), in_=consts_h[:]),
          reads=["consts"], writes=["consts"])

    p.op("v", lambda b: b.tensor_scalar(
        sb(OFF_HALFB, 16), sb(OFF_HALFB, 16), 0.5, None, ALU.mult),
        reads=["bbase"], writes=["bbase"])

    reg("Asb", OFF_ASB, 256)
    p.op("s", lambda b: b.activation(sb(OFF_ASB, 256), sb(OFF_TMPB, 256), AF.Exp),
         reads=["tmpbc"], writes=["Asb"])
    p.op("v", lambda b: b.tensor_scalar(
        sb(OFF_ASB, 256), sb(OFF_ASB, 256), -1.0, None, ALU.mult),
        reads=["Asb"], writes=["Asb"])

    reg("onesbf", OFF_ONESBF, 1)
    p.op("v", lambda b: b.memset(sbbf(OFF_ONESBF, 1), 1.0), writes=["onesbf"])
    reg("onesc", OFF_ONESC, 1)
    p.op("v", lambda b: b.memset(sb(OFF_ONESC, 1), 1.0), writes=["onesc"])

    STG0, STG1 = C0, C0 + 2048
    reg("stg0", STG0, 2048)
    reg("stg1", STG1, 2048)

    def load_cast(dst_off, n_elems, src_ap, src_buf, dst_buf, stg_off, stg_buf,
                  q="sy"):
        p.dma(q, lambda b: b.dma_start(out=sb(stg_off, n_elems), in_=src_ap),
              reads=[src_buf], writes=[stg_buf])
        p.op("v", lambda b: b.tensor_copy(
            sbbf(dst_off, (n_elems + 1) // 2)[:, 0:n_elems],
            sb(stg_off, n_elems)),
            reads=[stg_buf], writes=[dst_buf])

    for k in range(NT):
        reg(f"wbf{k}", B0 + 1024 * k, 1024)
        load_cast(B0 + 1024 * k, 2048, wb_h[128 * k:128 * (k + 1), :],
                  "wbase_t", f"wbf{k}", STG0 if k % 2 == 0 else STG1,
                  "stg0" if k % 2 == 0 else "stg1", q="p")
        p.dma("sc", lambda b, k=k: b.dma_start(
            out=wbsp[128 * k:128 * (k + 1), :],
            in_=sbbf(B0 + 1024 * k, 1024)[:, 0:2048]),
            reads=[f"wbf{k}"], writes=[f"wbsp{k}"])

    reg("tmpbc", OFF_TMPB, 1024)
    reg("a2bf", OFF_A2BF, 1024)
    p.dma("sy", lambda b: b.dma_start(out=sb(STG0, 2048), in_=a2_h[:]),
          reads=["a2_t"], writes=["stg0"])
    p.op("s", lambda b: b.activation(
        sbbf(OFF_A2BF, 1024)[:, 0:2048], sb(STG0, 2048), AF.Copy),
        reads=["stg0"], writes=["a2bf"])

    reg("bw2bf", OFF_BW2BF, 1024)
    p.dma("sy", lambda b: b.dma_start(out=sb(STG1, 2048), in_=bw2_h[:]),
          reads=["bw2_t"], writes=["stg1"])
    p.op("s", lambda b: b.activation(
        sbbf(OFF_BW2BF, 1024)[:, 0:2048], sb(STG1, 2048), AF.Copy),
        reads=["stg1"], writes=["bw2bf"])

    reg("rtrbf", OFF_RTRBF, 64)
    p.dma("sy", lambda b: b.dma_start(out=sb(OFF_TMPB, 128), in_=rtr_h[:]),
          reads=["router_t"], writes=["tmpbc"])
    p.op("s", lambda b: b.activation(
        sbbf(OFF_RTRBF, 64)[:, 0:128], sb(OFF_TMPB, 128), AF.Copy),
        reads=["tmpbc"], writes=["rtrbf"])

    reg("rb", OFF_RB, 1024)
    reg("rmb", OFF_RMB, 1024)
    reg("xbf", OFF_XBF, 512)
    reg("xsq", OFF_XSQ, 512)

    def emit_ln(x_off_fn, x_buf_fn, tag, skip_stats=False):
        sx = regp(f"ps_sx_{tag}", 0, 2)
        sxx = regp(f"ps_sxx_{tag}", 2, 2)
        for k in range(NT if not skip_stats else 0):
            xo = OFF_XBF if k % 2 == 0 else OFF_RB
            so = OFF_XSQ if k % 2 == 0 else OFF_RMB
            xb = "xbf" if k % 2 == 0 else "rb"
            sq = "xsq" if k % 2 == 0 else "rmb"
            p.op("s", lambda b, k=k, xo=xo: b.activation(
                sbbf(xo, 512)[:, 0:1024], sb(x_off_fn(k), 1024), AF.Copy),
                reads=[x_buf_fn(k)], writes=[xb])
            p.op("s", lambda b, xo=xo, so=so: b.activation(
                sbbf(so, 512)[:, 0:1024], sbbf(xo, 512)[:, 0:1024],
                AF.Square),
                reads=[xb], writes=[sq])
            for tc in range(2):
                p.op("te", lambda b, k=k, tc=tc, xo=xo: b.matmul(
                    pall[0:1, tc * 512:(tc + 1) * 512],
                    sbbf(OFF_ONESBF, 1)[:, 0:1],
                    sbbf(xo, 512)[:, tc * 512:(tc + 1) * 512],
                    start=(k == 0), stop=(k == NT - 1)),
                    reads=[xb, "onesbf"], writes=[sx], inc=False)
                p.op("te", lambda b, k=k, tc=tc, so=so: b.matmul(
                    pall[0:1, 1024 + tc * 512:1024 + (tc + 1) * 512],
                    sbbf(OFF_ONESBF, 1)[:, 0:1],
                    sbbf(so, 512)[:, tc * 512:(tc + 1) * 512],
                    start=(k == 0), stop=(k == NT - 1)),
                    reads=[sq, "onesbf"], writes=[sxx], inc=(tc == 1))
        p.op("v", lambda b: b.tensor_scalar(
            row_m[:], pall[0:1, 0:1024], 1.0 / D, None, ALU.mult),
            reads=[sx], writes=["row_m"])
        p.op("v", lambda b: b.tensor_scalar(
            row_e[:], pall[0:1, 1024:2048], 1.0 / D, None, ALU.mult),
            reads=[sxx], writes=["row_e"])
        p.dma("sy", lambda b: b.dma_start(out=stdr[0:1, :], in_=row_m[:]),
              reads=["row_m"], writes=["stdr"])
        p.dma("sy", lambda b: b.dma_start(out=stdr[1:2, :], in_=row_e[:]),
              reads=["row_e"], writes=["stdr"])
        p.dma("sy", lambda b: b.dma_start(
            out=sb(OFF_TMPB, 1024), in_=stdr[0:1, :].partition_broadcast(128)),
            reads=["stdr"], writes=["tmpbc"])
        p.dma("sy", lambda b: b.dma_start(
            out=sb(OFF_RB, 1024), in_=stdr[1:2, :].partition_broadcast(128)),
            reads=["stdr"], writes=["rb"])
        p.op("v", lambda b: b.tensor_tensor(
            sb(OFF_RMB, 1024), sb(OFF_TMPB, 1024), sb(OFF_TMPB, 1024), ALU.mult),
            reads=["tmpbc"], writes=["rmb"])
        p.op("v", lambda b: b.tensor_tensor(
            sb(OFF_RB, 1024), sb(OFF_RB, 1024), sb(OFF_RMB, 1024), ALU.subtract),
            reads=["rb", "rmb"], writes=["rb"])
        p.op("s", lambda b: b.activation(
            sb(OFF_RB, 1024), sb(OFF_RB, 1024), AF.Ln, bias=cst(C_EPS)),
            reads=["rb", "consts"], writes=["rb"])
        p.op("s", lambda b: b.activation(
            sb(OFF_RB, 1024), sb(OFF_RB, 1024), AF.Exp, scale=cst(C_NEGH)),
            reads=["rb", "consts"], writes=["rb"])
        p.op("v", lambda b: b.tensor_tensor(
            sb(OFF_RMB, 1024), sb(OFF_TMPB, 1024), sb(OFF_RB, 1024), ALU.mult),
            reads=["tmpbc", "rb"], writes=["rmb"])

    reg("lt", OFF_LT, 32)
    reg("mx", OFF_MX, 4)
    reg("maskT", OFF_MASKT, 32)
    reg("hall", OFF_HALL, 512)
    reg("hmbf", OFF_HMBF, 256)
    reg("t1a", OFF_T1A, 512)
    reg("sdelta", OFF_SDELTA, 1024)

    def emit_ffn(x_off_fn, x_buf_fn, xn_off, xn_buf_fn, out_write_fn, tag,
                 skip_stats=False):
        emit_ln(x_off_fn, x_buf_fn, tag, skip_stats=skip_stats)
        for tc in range(2):
            for k in range(NT):
                p.op("v", lambda b, k=k, tc=tc: b.tensor_tensor(
                    sb(OFF_TMPB, 512),
                    sb(x_off_fn(k), 1024)[:, tc * 512:(tc + 1) * 512],
                    sb(OFF_RB, 1024)[:, tc * 512:(tc + 1) * 512], ALU.mult),
                    reads=[x_buf_fn(k), "rb"], writes=["tmpbc"])
                p.op("v", lambda b, k=k, tc=tc: b.tensor_tensor(
                    sbbf(xn_off + 256 * k, 256)[:, 0:512], sb(OFF_TMPB, 512),
                    sb(OFF_RMB, 1024)[:, tc * 512:(tc + 1) * 512], ALU.subtract),
                    reads=["tmpbc", "rmb"], writes=[xn_buf_fn(k)])
            rps = regp("ps_rtr", 0, 1)
            for k in range(NT):
                p.op("te", lambda b, k=k, tc=tc: b.matmul(
                    pall[0:NEXP, 0:512],
                    sbbf(OFF_RTRBF, 64)[:, 8 * k:8 * (k + 1)],
                    sbbf(xn_off + 256 * k, 256)[:, 0:512],
                    start=(k == 0), stop=(k == NT - 1)),
                    reads=[xn_buf_fn(k), "rtrbf"], writes=[rps],
                    inc=(k == NT - 1))
            p.op("s", lambda b: b.copy(row_l[:], pall[0:NEXP, 0:512]),
                 reads=[rps], writes=["row_l"])
            tps = regp("ps_tr", 1, 1)
            for j in range(4):
                p.op("te", lambda b, j=j: b.transpose(
                    pall[:, 512:520], row_l[:, 128 * j:128 * (j + 1)],
                    arena[0:NEXP, OFF_IDENT:OFF_IDENT + NEXP]),
                    reads=["row_l", "ident"], writes=[tps])
                p.op("s", lambda b, j=j: b.copy(
                    arena[:, OFF_LT + 8 * j:OFF_LT + 8 * (j + 1)],
                    pall[:, 512:520]),
                    reads=[tps], writes=["lt"])
            p.op("v", lambda b: b.tensor_reduce(
                sb(OFF_MX, 4), sb(OFF_LT, 32).rearrange("p (j e) -> p j e", j=4),
                mybir.AxisListType.X, ALU.max),
                reads=["lt"], writes=["mx"])
            for j in range(4):
                p.op("v", lambda b, j=j: b.tensor_scalar(
                    arena[:, OFF_MASKT + 8 * j:OFF_MASKT + 8 * (j + 1)],
                    arena[:, OFF_LT + 8 * j:OFF_LT + 8 * (j + 1)],
                    arena[:, OFF_MX + j:OFF_MX + j + 1], None, ALU.is_ge),
                    reads=["lt", "mx"], writes=["maskT"])
            for j in range(4):
                p.op("te", lambda b, j=j: b.transpose(
                    pall[0:NEXP, 512:640],
                    arena[:, OFF_MASKT + 8 * j:OFF_MASKT + 8 * (j + 1)],
                    sb(OFF_IDENT, 128)),
                    reads=["maskT", "ident"], writes=[tps])
                p.op("s", lambda b, j=j: b.copy(
                    row_msk[:, 128 * j:128 * (j + 1)], pall[0:NEXP, 512:640]),
                    reads=[tps], writes=["row_msk"])
            mps = regp("ps_mer", 2, 1)
            p.op("te", lambda b: b.matmul(
                pall[:, 1024:1536], arena[0:NEXP, OFF_REP:OFF_REP + 128],
                row_msk[:], start=True, stop=True),
                reads=["row_msk", "repm"], writes=[mps])
            hps = regp("ps_hall", 3, 1)
            for k in range(NT):
                p.op("te", lambda b, k=k, tc=tc: b.matmul(
                    pall[:, 1536:2048],
                    sbbf(OFF_A2BF, 1024)[:, 128 * k:128 * (k + 1)],
                    sbbf(xn_off + 256 * k, 256)[:, 0:512],
                    start=(k == 0), stop=(k == NT - 1)),
                    reads=[xn_buf_fn(k), "a2bf"], writes=[hps],
                    inc=(k == NT - 1))
            p.op("s", lambda b: b.copy(sb(OFF_HALL, 512), pall[:, 1536:2048]),
                 reads=[hps], writes=["hall"])
            p.op("v", lambda b: b.tensor_tensor(
                sbbf(OFF_HMBF, 256)[:, 0:512], sb(OFF_HALL, 512),
                pall[:, 1024:1536], ALU.mult),
                reads=["hall", mps], writes=["hmbf"])
            for o in range(NT):
                bank = 4 + (o % 2)
                bps = regp(f"ps_base{o % 2}", bank, 1)
                t1off = OFF_T1A if o % 2 == 0 else OFF_SDELTA
                t1buf = "t1a" if o % 2 == 0 else "sdelta"
                for k in range(NT):
                    p.op("te", lambda b, o=o, k=k, tc=tc, bank=bank: b.matmul(
                        pall[:, bank * 512:(bank + 1) * 512],
                        sbbf(B0 + 1024 * k, 1024)[:, 128 * o:128 * (o + 1)],
                        sbbf(xn_off + 256 * k, 256)[:, 0:512],
                        start=(k == 0), stop=False),
                        reads=[xn_buf_fn(k), f"wbf{k}"], writes=[bps], inc=False)
                p.op("te", lambda b, o=o, bank=bank: b.matmul(
                    pall[:, bank * 512:(bank + 1) * 512],
                    sbbf(OFF_BW2BF, 1024)[:, 128 * o:128 * (o + 1)],
                    sbbf(OFF_HMBF, 256)[:, 0:512], start=False, stop=True),
                    reads=["hmbf", "bw2bf"], writes=[bps])
                p.op("s", lambda b, o=o, bank=bank, t1off=t1off: b.activation(
                    sb(t1off, 512), pall[:, bank * 512:(bank + 1) * 512],
                    AF.Identity, scale=cst(C_HALF),
                    bias=arena[:, OFF_HALFB + o:OFF_HALFB + o + 1]),
                    reads=[bps, "bbase", "consts"], writes=[t1buf])
                out_write_fn(o, tc, t1off, t1buf)

    def p1_out(o, tc, t1off, t1buf):
        p.op("v", lambda b: b.tensor_tensor(
            arena[:, A0 + 1024 * o + 512 * tc:A0 + 1024 * o + 512 * (tc + 1)],
            arena[:, A0 + 1024 * o + 512 * tc:A0 + 1024 * o + 512 * (tc + 1)],
            sb(t1off, 512), ALU.add),
            reads=[t1buf, f"xres{o}"], writes=[f"xres{o}"])

    for k in range(NT):
        reg(f"xn1bf{k}", C0 + 256 * k, 256)
    emit_ffn(lambda k: A0 + 1024 * k, lambda k: f"xres{k}",
             C0, lambda k: f"xn1bf{k}", p1_out, "ln1")

    for k in range(NT):
        p.dma("sy", lambda b, k=k: b.dma_start(
            out=x1sp[128 * k:128 * (k + 1), :], in_=sb(A0 + 1024 * k, 1024)),
            reads=[f"xres{k}"], writes=[f"x1sp{k}"])

    sxt = regp("ps_sxt", 6, 1)
    sxxt = regp("ps_sxxt", 7, 1)
    for q in range(2):
        reg(f"tl{q}", OFF_XBF + 2 * q, 2)
        reg(f"ts{q}", OFF_XSQ + 2 * q, 2)
    for k in range(NT):
        xo, so = OFF_XBF + 2 * (k % 2), OFF_XSQ + 2 * (k % 2)
        xb, sq = f"tl{k % 2}", f"ts{k % 2}"
        p.op("s", lambda b, k=k, xo=xo: b.activation(
            sbbf(xo, 2)[:, 0:3],
            arena[:, A0 + 1024 * k + 1021:A0 + 1024 * k + 1024], AF.Copy),
            reads=[f"xres{k}"], writes=[xb])
        p.op("s", lambda b, xo=xo, so=so: b.activation(
            sbbf(so, 2)[:, 0:3], sbbf(xo, 2)[:, 0:3], AF.Square),
            reads=[xb], writes=[sq])
        p.op("te", lambda b, k=k, xo=xo: b.matmul(
            pall[0:1, 3072:3075], sbbf(OFF_ONESBF, 1)[:, 0:1],
            sbbf(xo, 2)[:, 0:3], start=(k == 0), stop=(k == NT - 1)),
            reads=[xb, "onesbf"], writes=[sxt])
        p.op("te", lambda b, k=k, so=so: b.matmul(
            pall[0:1, 3584:3587], sbbf(OFF_ONESBF, 1)[:, 0:1],
            sbbf(so, 2)[:, 0:3], start=(k == 0), stop=(k == NT - 1)),
            reads=[sq, "onesbf"], writes=[sxxt])
    p.op("v", lambda b: b.tensor_scalar(
        row_m[:, 0:3], pall[0:1, 3072:3075], 1.0 / D, None, ALU.mult),
        reads=[sxt], writes=["row_m"])
    p.op("v", lambda b: b.tensor_scalar(
        row_e[:, 0:3], pall[0:1, 3584:3587], 1.0 / D, None, ALU.mult),
        reads=[sxxt], writes=["row_e"])
    p.op("v", lambda b: b.tensor_tensor(
        row_m[:, 4:7], row_m[:, 0:3], row_m[:, 0:3], ALU.mult),
        reads=["row_m"], writes=["row_m"])
    p.op("v", lambda b: b.tensor_tensor(
        row_e[:, 0:3], row_e[:, 0:3], row_m[:, 4:7], ALU.subtract),
        reads=["row_e", "row_m"], writes=["row_e"])
    p.op("s", lambda b: b.activation(
        row_e[:, 0:3], row_e[:, 0:3], AF.Ln, bias=cst(C_EPS)[0:1, :]),
        reads=["row_e", "consts"], writes=["row_e"])
    p.op("s", lambda b: b.activation(
        row_e[:, 0:3], row_e[:, 0:3], AF.Exp, scale=cst(C_NEGH)[0:1, :]),
        reads=["row_e", "consts"], writes=["row_e"])
    p.op("v", lambda b: b.tensor_tensor(
        row_m[:, 4:7], row_m[:, 0:3], row_e[:, 0:3], ALU.mult),
        reads=["row_m", "row_e"], writes=["row_m"])
    p.dma("sy", lambda b: b.dma_start(out=stdt[0:1, 0:3], in_=row_e[:, 0:3]),
          reads=["row_e"], writes=["stdt"])
    p.dma("sy", lambda b: b.dma_start(out=stdt[1:2, 0:3], in_=row_m[:, 4:7]),
          reads=["row_m"], writes=["stdt"])
    reg("rt3", OFF_RT3, 3)
    reg("mrt3", OFF_MRT3, 3)
    p.dma("sy", lambda b: b.dma_start(
        out=sb(OFF_RT3, 3), in_=stdt[0:1, 0:3].partition_broadcast(128)),
        reads=["stdt"], writes=["rt3"])
    p.dma("sy", lambda b: b.dma_start(
        out=sb(OFF_MRT3, 3), in_=stdt[1:2, 0:3].partition_broadcast(128)),
        reads=["stdt"], writes=["mrt3"])
    reg("tails", OFF_TAILS, 48)
    for k in range(NT):
        p.op("v", lambda b, k=k: b.tensor_tensor(
            arena[:, OFF_TAILS + 3 * k:OFF_TAILS + 3 * (k + 1)],
            arena[:, A0 + 1024 * k + 1021:A0 + 1024 * k + 1024],
            sb(OFF_RT3, 3), ALU.mult),
            reads=[f"xres{k}", "rt3"], writes=["tails"])
        p.op("v", lambda b, k=k: b.tensor_tensor(
            arena[:, OFF_TAILS + 3 * k:OFF_TAILS + 3 * (k + 1)],
            arena[:, OFF_TAILS + 3 * k:OFF_TAILS + 3 * (k + 1)],
            sb(OFF_MRT3, 3), ALU.subtract),
            reads=["tails", "mrt3"], writes=["tails"])
    p.dma("p", lambda b: b.dma_start(
        out=halo_in.rearrange("(k p) c -> p k c", p=128),
        in_=sb(OFF_TAILS, 48).rearrange("p (k c) -> p k c", k=NT)),
        reads=["tails"], writes=["halo_in"])
    p.cc(lambda b: b.collective_compute(
        "AllGather", ALU.bypass, replica_groups=[list(range(NCORES))],
        ins=[halo_in.opt()], outs=[halo_out.opt()]),
        reads=["halo_in"], writes=["halo_out"])

    emit_ln(lambda k: A0 + 1024 * k, lambda k: f"xres{k}", "ln2")

    reg("pstg", OFF_XN2, 2048)
    reg("xpjbf", OFF_XPJBF, 1280)
    for hh in range(2):
        p.dma("sy", lambda b, hh=hh: b.dma_start(
            out=sb(OFF_XN2, 1280), in_=xp_h[:, 1280 * hh:1280 * (hh + 1)]),
            reads=["xproj_t"], writes=["pstg"])
        p.op("s", lambda b, hh=hh: b.activation(
            sbbf(OFF_XPJBF + 640 * hh, 640)[:, 0:1280], sb(OFF_XN2, 1280), AF.Copy),
            reads=["pstg"], writes=["xpjbf"])
    reg("dtpjbf", OFF_DTPJBF, 1024)
    p.dma("sy", lambda b: b.dma_start(out=sb(OFF_XN2, 2048), in_=dtp_h[:]),
          reads=["dtproj_t"], writes=["pstg"])
    p.op("s", lambda b: b.activation(
        sbbf(OFF_DTPJBF, 1024)[:, 0:2048], sb(OFF_XN2, 2048), AF.Copy),
        reads=["pstg"], writes=["dtpjbf"])


    dtps = regp("ps_dt", 0, 2)
    bps_ = regp("ps_B", 2, 2)
    cps_ = regp("ps_C", 4, 2)
    reg("xn2", OFF_XN2, 1024)
    reg("uacc", OFF_UACC, 1024)
    reg("halo8", OFF_HALO8, 24)
    reg("hx", OFF_HX, 8)
    for k in range(NT):
        ua = A0 + 1024 * k
        ub = f"xres{k}"
        p.op("v", lambda b, k=k: b.tensor_tensor(
            sb(OFF_XN2, 1024), sb(A0 + 1024 * k, 1024), sb(OFF_RB, 1024),
            ALU.mult),
            reads=[f"xres{k}", "rb"], writes=["xn2"])
        p.op("v", lambda b: b.tensor_tensor(
            sb(OFF_XN2, 1024), sb(OFF_XN2, 1024), sb(OFF_RMB, 1024),
            ALU.subtract),
            reads=["xn2", "rmb"], writes=["xn2"])
        p.dma("p", lambda b, k=k: b.dma_start(
            out=sb(OFF_HALO8, 24).rearrange("p (j c) -> p j c", j=8),
            in_=halo_out.rearrange("(j k2 p) c -> p j k2 c", j=8, p=128)[:, :, k, :]),
            reads=["halo_out"], writes=["halo8"])
        p.op("v", lambda b, k=k, ua=ua: b.tensor_scalar(
            arena[:, ua + 3:ua + 1024],
            arena[:, OFF_XN2 + 3:OFF_XN2 + 1024],
            arena[:, OFF_CONVW + 4 * k + 3:OFF_CONVW + 4 * k + 4],
            arena[:, OFF_CONVB + k:OFF_CONVB + k + 1], ALU.mult, ALU.add),
            reads=["xn2", "convw", "convb"], writes=[ub])
        for j in (2, 1, 0):
            p.op("v", lambda b, k=k, j=j, ua=ua: b.scalar_tensor_tensor(
                arena[:, ua + 3:ua + 1024],
                arena[:, OFF_XN2 + j:OFF_XN2 + j + 1021],
                arena[:, OFF_CONVW + 4 * k + j:OFF_CONVW + 4 * k + j + 1],
                arena[:, ua + 3:ua + 1024], ALU.mult, ALU.add),
                reads=["xn2", "convw", ub], writes=[ub])
        p.op("v", lambda b: b.tensor_tensor(
            sb(OFF_HALO8, 24), sb(OFF_HALO8, 24),
            arena[:, OFF_PREV24:OFF_PREV24 + 24], ALU.mult),
            reads=["halo8", "prev24"], writes=["halo8"])
        p.op("v", lambda b: b.tensor_reduce(
            sb(OFF_HX, 3), sb(OFF_HALO8, 24).rearrange(
                "p (j c) -> p c j", c=3) if False else
            sb(OFF_HALO8, 24).rearrange("p (j c) -> p j c", j=8).transpose(
                [0, 2, 1]),
            mybir.AxisListType.X, ALU.add),
            reads=["halo8"], writes=["hx"])
        p.op("v", lambda b: b.tensor_copy(
            arena[:, OFF_HX + 3:OFF_HX + 6], sb(OFF_XN2, 3)),
            reads=["xn2"], writes=["hx"])
        p.op("v", lambda b, k=k, ua=ua: b.tensor_scalar(
            arena[:, ua:ua + 3], arena[:, OFF_HX + 3:OFF_HX + 6],
            arena[:, OFF_CONVW + 4 * k + 3:OFF_CONVW + 4 * k + 4],
            arena[:, OFF_CONVB + k:OFF_CONVB + k + 1], ALU.mult, ALU.add),
            reads=["hx", "convw", "convb"], writes=[ub])
        for j in (2, 1, 0):
            p.op("v", lambda b, k=k, j=j, ua=ua: b.scalar_tensor_tensor(
                arena[:, ua:ua + 3], arena[:, OFF_HX + j:OFF_HX + j + 3],
                arena[:, OFF_CONVW + 4 * k + j:OFF_CONVW + 4 * k + j + 1],
                arena[:, ua:ua + 3], ALU.mult, ALU.add),
                reads=["hx", "convw", ub], writes=[ub])
        p.dma("sy", lambda b, k=k, ua=ua: b.dma_start(
            out=usp[128 * k:128 * (k + 1), :], in_=sb(ua, 1024)),
            reads=[ub], writes=[f"usp{k}"])
        p.op("s", lambda b, ua=ua: b.activation(
            sbbf(OFF_XBF, 512)[:, 0:1024], sb(ua, 1024), AF.Copy),
            reads=[ub], writes=["xbf"])
        for tc in range(2):
            p.op("te", lambda b, k=k, tc=tc: b.matmul(
                pall[:, tc * 512:(tc + 1) * 512],
                sbbf(OFF_XPJBF, 1280)[:, 160 * k:160 * k + 128],
                sbbf(OFF_XBF, 512)[:, tc * 512:(tc + 1) * 512],
                start=(k == 0), stop=(k == NT - 1)),
                reads=["xbf", "xpjbf"], writes=[dtps], inc=False)
            p.op("te", lambda b, k=k, tc=tc: b.matmul(
                pall[0:NSTATE, 1024 + tc * 512:1024 + (tc + 1) * 512],
                sbbf(OFF_XPJBF, 1280)[:, 160 * k + 128:160 * k + 144],
                sbbf(OFF_XBF, 512)[:, tc * 512:(tc + 1) * 512],
                start=(k == 0), stop=(k == NT - 1)),
                reads=["xbf", "xpjbf"], writes=[bps_], inc=False)
            p.op("te", lambda b, k=k, tc=tc: b.matmul(
                pall[0:NSTATE, 2048 + tc * 512:2048 + (tc + 1) * 512],
                sbbf(OFF_XPJBF, 1280)[:, 160 * k + 144:160 * k + 160],
                sbbf(OFF_XBF, 512)[:, tc * 512:(tc + 1) * 512],
                start=(k == 0), stop=(k == NT - 1)),
                reads=["xbf", "xpjbf"], writes=[cps_], inc=(tc == 1))

    reg("dtbf", OFF_DTBF, 512)
    p.op("s", lambda b: b.activation(
        sbbf(OFF_DTBF, 512)[:, 0:1024], pall[:, 0:1024], AF.Copy),
        reads=[dtps], writes=["dtbf"])
    p.op("s", lambda b: b.copy(row_bc[:], pall[0:NSTATE, 1024:2048]),
         reads=[bps_], writes=["row_bc"])
    p.dma("sy", lambda b: b.dma_start(out=bcB[:], in_=row_bc[:]),
          reads=["row_bc"], writes=["bcB"])
    p.op("s", lambda b: b.copy(row_bc[:], pall[0:NSTATE, 2048:3072]),
         reads=[cps_], writes=["row_bc"])
    p.dma("sy", lambda b: b.dma_start(out=bcC[:], in_=row_bc[:]),
          reads=["row_bc"], writes=["bcC"])
    for i in range(NH):
        reg(f"Bmb{i}", B0 + 1024 * i, 1024)
        reg(f"Cb{i}", B0 + 8192 + 1024 * i, 1024)
        p.dma("p", lambda b, i=i: b.dma_start(
            out=sb(B0 + 1024 * i, 1024),
            in_=bcB[i:i + 1, :].partition_broadcast(128)),
            reads=["bcB"], writes=[f"Bmb{i}"])
        p.dma("p", lambda b, i=i: b.dma_start(
            out=sb(B0 + 8192 + 1024 * i, 1024),
            in_=bcC[i:i + 1, :].partition_broadcast(128)),
            reads=["bcC"], writes=[f"Cb{i}"])

    for k in range(NT):
        if k % 2 == 0:
            dps, dc0 = regp("ps_dpre", 6, 2), 3072
        else:
            dps, dc0 = regp("ps_dpreB", 2, 2), 1024
        for tc in range(2):
            p.op("te", lambda b, k=k, tc=tc, dc0=dc0: b.matmul(
                pall[:, dc0 + tc * 512:dc0 + (tc + 1) * 512],
                sbbf(OFF_DTPJBF, 1024)[:, 128 * k:128 * (k + 1)],
                sbbf(OFF_DTBF, 512)[:, tc * 512:(tc + 1) * 512],
                start=True, stop=True),
                reads=["dtbf", "dtpjbf"], writes=[dps])
        p.op("v", lambda b, k=k, dc0=dc0: b.tensor_scalar(
            sb(OFF_UACC, 1024), pall[:, dc0:dc0 + 1024],
            arena[:, OFF_DTPB + k:OFF_DTPB + k + 1], 0.0,
            ALU.add, ALU.max),
            reads=[dps, "dtpb"], writes=["uacc"])
        p.op("s", lambda b, k=k, dc0=dc0: b.activation(
            sb(OFF_XN2, 1024), pall[:, dc0:dc0 + 1024], AF.Abs,
            bias=arena[:, OFF_DTPB + k:OFF_DTPB + k + 1]),
            reads=[dps, "dtpb"], writes=["xn2"])
        p.op("s", lambda b: b.activation(
            sb(OFF_XN2, 1024), sb(OFF_XN2, 1024), AF.Exp, scale=cst(C_NEG1)),
            reads=["xn2", "consts"], writes=["xn2"])
        p.op("s", lambda b: b.activation(
            sb(OFF_XN2, 1024), sb(OFF_XN2, 1024), AF.Ln, bias=cst(C_ONE)),
            reads=["xn2", "consts"], writes=["xn2"])
        p.op("v", lambda b: b.tensor_tensor(
            sb(OFF_XN2, 1024), sb(OFF_XN2, 1024), sb(OFF_UACC, 1024), ALU.add),
            reads=["xn2", "uacc"], writes=["xn2"])
        p.dma("sy", lambda b, k=k: b.dma_start(
            out=dsp[128 * k:128 * (k + 1), :], in_=sb(OFF_XN2, 1024)),
            reads=["xn2"], writes=[f"dsp{k}"])

    for r in range(12):
        reg(f"ring{r}", RING[r], 1024)
    for i in range(NH):
        reg(f"Bmb{i}", B0 + 1024 * i, 1024)
        reg(f"Cb{i}", B0 + 8192 + 1024 * i, 1024)
    for par in range(2):
        reg(f"e0p{par}", E0P[par], 1024)
        reg(f"delta{par}", OFF_DELTA2[par], 1024)
        reg(f"ut{par}", OFF_UT2[par], 1024)
    reg("dbu", OFF_DB, 2048)
    reg("du", OFF_DU, 1024)
    reg("yb", OFF_Y, 1024)
    reg("x1l", OFF_X1L, 1024)
    reg("gathA", OFF_GATH, 64)
    reg("gathB", OFF_GATH + 64, 64)
    reg("summA", OFF_SUMM, 8)
    reg("summB", OFF_SUMM + 8, 8)
    reg("hinA", OFF_HIN, 4)
    reg("hinB", OFF_HIN + 4, 4)
    reg("tch", OFF_TCH, 64)
    reg("tsc", OFF_TSC, 64)

    def p3_build_head(H, k):
        par = k % 2
        dlt, dbuf = OFF_DELTA2[par], f"delta{par}"
        utt, ubuf = OFF_UT2[par], f"ut{par}"
        p.dma("sy", lambda b, k=k, dlt=dlt: b.dma_start(
            out=sb(dlt, 1024), in_=dsp[128 * k:128 * (k + 1), :]),
            reads=[f"dsp{k}"], writes=[dbuf])
        p.dma("sy", lambda b, k=k, utt=utt: b.dma_start(
            out=sb(utt, 1024), in_=usp[128 * k:128 * (k + 1), :]),
            reads=[f"usp{k}"], writes=[ubuf])
        p.op("v", lambda b, dlt=dlt, utt=utt: b.tensor_tensor(
            sb(OFF_DU, 1024), sb(dlt, 1024), sb(utt, 1024), ALU.mult),
            reads=[dbuf, ubuf], writes=["du"])
        if H == 0:
            p.op("v", lambda b, dlt=dlt: b.tensor_reduce(
                arena[:, OFF_TCH:OFF_TCH + 1], sb(dlt, 1024),
                mybir.AxisListType.X, ALU.add),
                reads=[dbuf], writes=["tch"])
            p.op("s", lambda b, k=k, H=H: b.activation(
                arena[:, OFF_SUMM + 8:OFF_SUMM + 16],
                arena[:, OFF_ASB + 16 * k + 8 * H:OFF_ASB + 16 * k + 8 * H + 8],
                AF.Exp, scale=arena[:, OFF_TCH:OFF_TCH + 1]),
                reads=["Asb", "tch"], writes=["summB"])

    def p3_build_step(H, k, i):
        par = k % 2
        dlt, dbuf = OFF_DELTA2[par], f"delta{par}"
        r0 = (8 * k + i) % 12
        rs = RING[r0]
        rdeps = [f"ring{r0}", f"ring{r0 + 1}"]
        for q in range(2):
            n = H * NH + i + q
            p.op("s", lambda b, k=k, n=n, q=q, rs=rs, dlt=dlt: b.activation(
                sb(rs + 1024 * q, 1024), sb(dlt, 1024), AF.Exp,
                scale=arena[:, OFF_ASB + 16 * k + n:OFF_ASB + 16 * k + n + 1]),
                reads=[dbuf, "Asb"], writes=[rdeps[q]])
        p.op("v", lambda b, rs=rs: b.memset(arena[:, rs + 1024:rs + 1025], 0.0),
            reads=[], writes=[rdeps[1]])
        p.op("v", lambda b, i=i: b.tensor_tensor(
            sb(OFF_DB, 2048).rearrange("p (w t) -> p w t", w=2),
            sb(OFF_DU, 1024).rearrange("p (w t) -> p w t", w=1)
            .to_broadcast([128, 2, 1024]),
            sb(B0 + 1024 * i, 2048).rearrange("p (w t) -> p w t", w=2),
            ALU.mult),
            reads=["du", f"Bmb{i}", f"Bmb{i + 1}"], writes=["dbu"])
        p.op("v", lambda b, rs=rs: b.tensor_tensor_scan(
            sb(rs, 2048), sb(rs, 2048), sb(OFF_DB, 2048), 0.0,
            ALU.mult, ALU.add),
            reads=rdeps + ["dbu"], writes=rdeps)

    def p3_gather_a(H, k):
        g0 = (8 * k) % 12
        runs = [(g0, 0, 8)] if g0 + 8 <= 12 else [(g0, 0, 12 - g0),
                                                  (0, 12 - g0, 8 - (12 - g0))]
        rdeps = [f"ring{(8 * k + q) % 12}" for q in range(8)]
        for (rstart, c0, ln) in runs:
            p.op("v", lambda b, rstart=rstart, c0=c0, ln=ln: b.tensor_copy(
                arena[:, OFF_SUMM + c0:OFF_SUMM + c0 + ln]
                .rearrange("p (i o) -> p i o", o=1),
                arena[:, A0 + 1024 * rstart:A0 + 1024 * (rstart + ln)]
                .rearrange("p (i t) -> p i t", i=ln)[:, :, 1023:1024]),
                reads=rdeps, writes=["summA"])

    def p3_gather_b(H, k):
        r = H * NT + k
        p.dma("p", lambda b, r=r: b.dma_start(
            out=ccin[r][:], in_=sb(OFF_SUMM, 16)),
            reads=["summA", "summB"], writes=[f"ccin{r}"])
        p.cc(lambda b, r=r: b.collective_compute(
            "AllGather", ALU.bypass, replica_groups=[list(range(NCORES))],
            ins=[ccin[r].opt()], outs=[ccout[r].opt()]),
            reads=[f"ccin{r}"], writes=[f"ccout{r}"])
        p.dma("p", lambda b, r=r: b.dma_start(
            out=sb(OFF_GATH, 128).rearrange("p (j c) -> p j c", j=8),
            in_=ccout[r].rearrange("(j p) c -> p j c", p=128)),
            reads=[f"ccout{r}"], writes=["gathA"])

    def p3_tchain(H, j):
        gv = arena[:, OFF_GATH:OFF_GATH + 128].rearrange(
            "p (j2 h n) -> p j2 h n", j2=8, h=2)
        p.op("v", lambda b, gv=gv: b.tensor_tensor(
            sb(OFF_TCH, 64).rearrange("p (n j2) -> p n j2", n=8),
            gv[:, :, 1, :].transpose([0, 2, 1]),
            sb(OFF_CM8, 64).rearrange("p (n j2) -> p n j2", n=8),
            ALU.mult),
            reads=["gathA", "cm8"], writes=["tch"])
        p.op("v", lambda b, gv=gv: b.tensor_copy(
            sb(OFF_TSC, 64).rearrange("p (n j2) -> p n j2", n=8),
            gv[:, :, 0, :].transpose([0, 2, 1])),
            reads=["gathA"], writes=["tsc"])
        p.op("v", lambda b: b.tensor_tensor_scan(
            sb(OFF_TSC, 64), sb(OFF_TCH, 64), sb(OFF_TSC, 64),
            0.0, ALU.mult, ALU.add),
            reads=["tch", "tsc"], writes=["tsc"])
        p.op("v", lambda b: b.tensor_tensor(
            sb(OFF_TCH, 64), sb(OFF_TSC, 64),
            arena[:, OFF_PREV8:OFF_PREV8 + 64], ALU.mult),
            reads=["tsc", "prev8"], writes=["tch"])
        p.op("v", lambda b: b.tensor_reduce(
            sb(OFF_HIN, 8), sb(OFF_TCH, 64).rearrange("p (n j2) -> p n j2", n=8),
            mybir.AxisListType.X, ALU.add),
            reads=["tch"], writes=["hinA"])

    def p3_post_step(H, j, i):
        par = j % 2
        dlt, dbuf = OFF_DELTA2[par], f"delta{par}"
        r0 = (8 * j + i) % 12
        rs = RING[r0]
        rdeps = [f"ring{r0}", f"ring{r0 + 1}"]
        hname = "hinA"
        for q in range(2):
            n = H * NH + i + q
            e0 = E0P[q]
            p.op("s", lambda b, j=j, n=n, e0=e0, dlt=dlt: b.activation(
                sb(e0, 1024), sb(dlt, 1024), AF.Exp,
                scale=arena[:, OFF_ASB + 16 * j + n:OFF_ASB + 16 * j + n + 1]),
                reads=[dbuf, "Asb"], writes=[f"e0p{q}"])
            p.op("v", lambda b, i=i, q=q, e0=e0, rs=rs: b.scalar_tensor_tensor(
                sb(rs + 1024 * q, 1024), sb(e0, 1024),
                arena[:, OFF_HIN + i + q:OFF_HIN + i + q + 1],
                sb(rs + 1024 * q, 1024), ALU.mult, ALU.add),
                reads=[f"e0p{q}", hname, rdeps[q]], writes=[rdeps[q]])
        p.op("v", lambda b, i=i, rs=rs: b.tensor_tensor(
            sb(rs, 2048), sb(rs, 2048),
            sb(B0 + 8192 + 1024 * i, 2048), ALU.mult),
            reads=rdeps + [f"Cb{i}", f"Cb{i + 1}"], writes=rdeps)
        if H == 0 and i == 0:
            p.op("v", lambda b, rs=rs: b.tensor_tensor(
                sb(OFF_Y, 1024), sb(rs, 1024), sb(rs + 1024, 1024), ALU.add),
                reads=rdeps, writes=["yb"])
        else:
            for q in range(2):
                p.op("v", lambda b, q=q, rs=rs: b.tensor_tensor(
                    sb(OFF_Y, 1024), sb(OFF_Y, 1024),
                    sb(rs + 1024 * q, 1024), ALU.add),
                    reads=["yb", rdeps[q]], writes=["yb"])

    def p3_y_init(H, j):
        if H == 1:
            p.dma("sy", lambda b, j=j: b.dma_start(
                out=sb(OFF_Y, 1024), in_=ysp[128 * j:128 * (j + 1), :]),
                reads=[f"ysp{j}"], writes=["yb"])

    def p3_sdelta(H, k):
        par = k % 2
        dlt, dbuf = OFF_DELTA2[par], f"delta{par}"
        if H == 0:
            p.op("v", lambda b, dlt=dlt: b.tensor_tensor_scan(
                sb(dlt, 1024), sb(OFF_ONESC, 1).to_broadcast([128, 1024]),
                sb(dlt, 1024), 0.0, ALU.mult, ALU.add),
                reads=["onesc", dbuf], writes=[dbuf])
            p.dma("sc", lambda b, k=k, dlt=dlt: b.dma_start(
                out=sdsp[128 * k:128 * (k + 1), :], in_=sb(dlt, 1024)),
                reads=[dbuf], writes=[f"sdsp{k}"])
        else:
            p.dma("sc", lambda b, k=k, dlt=dlt: b.dma_start(
                out=sb(dlt, 1024), in_=sdsp[128 * k:128 * (k + 1), :]),
                reads=[f"sdsp{k}"], writes=[dbuf])

    def p3_d8_late(k):
        dlt, dbuf = OFF_DELTA2[k % 2], f"delta{k % 2}"
        p.op("s", lambda b, k=k, dlt=dlt: b.activation(
            arena[:, OFF_SUMM + 8:OFF_SUMM + 16],
            arena[:, OFF_ASB + 16 * k + 8:OFF_ASB + 16 * k + 16],
            AF.Exp, scale=arena[:, dlt + 1023:dlt + 1024]),
            reads=["Asb", dbuf], writes=["summB"])

    def p3_post_tail(H, j):
        par = j % 2
        utt, ubuf = OFF_UT2[par], f"ut{par}"
        if H == 0:
            p.dma("p", lambda b, j=j: b.dma_start(
                out=ysp[128 * j:128 * (j + 1), :], in_=sb(OFF_Y, 1024)),
                reads=["yb"], writes=[f"ysp{j}"])
        else:
            p.op("v", lambda b, j=j, utt=utt: b.scalar_tensor_tensor(
                sb(OFF_Y, 1024), sb(utt, 1024),
                arena[:, OFF_DSKIP + j:OFF_DSKIP + j + 1], sb(OFF_Y, 1024),
                ALU.mult, ALU.add),
                reads=[ubuf, "dskip", "yb"], writes=["yb"])
            p.dma("sy", lambda b, j=j: b.dma_start(
                out=sb(OFF_X1L, 1024), in_=x1sp[128 * j:128 * (j + 1), :]),
                reads=[f"x1sp{j}"], writes=["x1l"])
            p.op("v", lambda b: b.tensor_tensor(
                sb(OFF_Y, 1024), sb(OFF_Y, 1024), sb(OFF_X1L, 1024), ALU.add),
                reads=["yb", "x1l"], writes=["yb"])
            p.dma("p", lambda b, j=j: b.dma_start(
                out=x2sp[128 * j:128 * (j + 1), :], in_=sb(OFF_Y, 1024)),
                reads=["yb"], writes=[f"x2sp{j}"])
            p.op("s", lambda b: b.activation(
                sbbf(OFF_SDELTA, 512)[:, 0:1024], sb(OFF_Y, 1024), AF.Copy),
                reads=["yb"], writes=["sdelta"])
            p.op("s", lambda b: b.activation(
                sbbf(OFF_SDELTA + 512, 512)[:, 0:1024],
                sbbf(OFF_SDELTA, 512)[:, 0:1024], AF.Square),
                reads=["sdelta"], writes=["sdelta"])
            for tc3 in range(2):
                p.op("te", lambda b, j=j, tc3=tc3: b.matmul(
                    pall[0:1, tc3 * 512:(tc3 + 1) * 512],
                    sbbf(OFF_ONESBF, 1)[:, 0:1],
                    sbbf(OFF_SDELTA, 512)[:, tc3 * 512:(tc3 + 1) * 512],
                    start=(j == 0), stop=(j == NT - 1)),
                    reads=["sdelta", "onesbf"], writes=[regp("ps_sx_ln3", 0, 2)],
                    inc=False)
                p.op("te", lambda b, j=j, tc3=tc3: b.matmul(
                    pall[0:1, 1024 + tc3 * 512:1024 + (tc3 + 1) * 512],
                    sbbf(OFF_ONESBF, 1)[:, 0:1],
                    sbbf(OFF_SDELTA + 512, 512)[:, tc3 * 512:(tc3 + 1) * 512],
                    start=(j == 0), stop=(j == NT - 1)),
                    reads=["sdelta", "onesbf"], writes=[regp("ps_sxx_ln3", 2, 2)],
                    inc=(tc3 == 1))

    def p3_bcast(H):
        for i in range(NH):
            n = H * NH + i
            p.dma("p", lambda b, n=n, i=i: b.dma_start(
                out=sb(B0 + 1024 * i, 1024),
                in_=bcB[n:n + 1, :].partition_broadcast(128)),
                reads=["bcB"], writes=[f"Bmb{i}"])
            p.dma("p", lambda b, n=n, i=i: b.dma_start(
                out=sb(B0 + 8192 + 1024 * i, 1024),
                in_=bcC[n:n + 1, :].partition_broadcast(128)),
                reads=["bcC"], writes=[f"Cb{i}"])

    for H in range(2):
        for ph in range(NT + 1):
            build = ph < NT
            post = ph >= 1
            if build:
                p3_build_head(H, ph)
            if post:
                p3_tchain(H, ph - 1)
                p3_y_init(H, ph - 1)
            if build:
                p3_build_step(H, ph, 0)
            for i in range(0, 6, 2):
                if post:
                    p3_post_step(H, ph - 1, i)
                if build:
                    p3_build_step(H, ph, i + 2)
            if build and H == 1:
                p3_sdelta(H, ph)
                p3_gather_a(H, ph)
                if post:
                    p3_post_step(H, ph - 1, 6)
                p3_d8_late(ph)
                p3_gather_b(H, ph)
            elif build:
                p3_gather_a(H, ph)
                p3_gather_b(H, ph)
                if post:
                    p3_post_step(H, ph - 1, 6)
                p3_sdelta(H, ph)
            elif post:
                p3_post_step(H, ph - 1, 6)
            if H == 0 and ph == NT:
                p3_bcast(1)
            if post:
                p3_post_tail(H, ph - 1)

    for k in range(NT):
        p.dma("sy" if k % 2 == 0 else "sc", lambda b, k=k: b.dma_start(
            out=sbbf(B0 + 1024 * k, 1024)[:, 0:2048],
            in_=wbsp[128 * k:128 * (k + 1), :]),
            reads=[f"wbsp{k}"], writes=[f"wbf{k}"])

    for k in range(NT):
        p.dma("p", lambda b, k=k: b.dma_start(
            out=sb(A0 + 1024 * k, 1024), in_=x2sp[128 * k:128 * (k + 1), :]),
            reads=[f"x2sp{k}"], writes=[f"xres{k}"])
    def p4_out(o, tc, t1off, t1buf):
        p.op("v", lambda b, o=o, tc=tc, t1off=t1off: b.tensor_tensor(
            sb(t1off, 512),
            arena[:, A0 + 1024 * o + 512 * tc:A0 + 1024 * o + 512 * (tc + 1)],
            sb(t1off, 512), ALU.add),
            reads=[t1buf, f"xres{o}"], writes=[t1buf])
        p.dma("p", lambda b, o=o, tc=tc, t1off=t1off: b.dma_start(
            out=out_h[128 * o:128 * (o + 1), 512 * tc:512 * (tc + 1)],
            in_=sb(t1off, 512)),
            reads=[t1buf], writes=["out"])

    for k in range(NT):
        reg(f"xn3bf{k}", C0 + 256 * k, 256)
    emit_ffn(lambda k: A0 + 1024 * k, lambda k: f"xres{k}",
             C0, lambda k: f"xn3bf{k}", p4_out, "ln3", skip_stats=True)

    final = [(s, p.cnt[s])
             for s in ("dsy", "dsy1", "dsy2", "dgp", "dgp1", "dgp2",
                       "dsc", "dsc1", "dsc2")
             if s in p.cnt]
    final += [(s, 1) for s in p.ccnames]
    p.wait_only("sy", final)

    p.emit(nc, stack)
    return nc, stack


_CACHE = {}


def kernel(L0_attn_x, L0_attn_conv_w, L0_attn_conv_b, L0_attn_xproj_w,
           L0_attn_dtproj_w, L0_attn_dtproj_b, L0_attn_A_log, L0_attn_D_skip,
           L0_ffn_W_base, L0_ffn_b_base, L0_ffn_router_W, L0_ffn_A, L0_ffn_Bw):
    if "nc" not in _CACHE:
        nc, stack = build_program()
        _CACHE["nc"] = nc
        _CACHE["stack"] = stack
    nc = _CACHE["nc"]

    f32 = np.float32

    def tilemaj(v):
        C = v.shape[1]
        return np.ascontiguousarray(
            v.reshape(NT, 128, C).transpose(1, 0, 2).reshape(128, NT * C))

    x = np.asarray(L0_attn_x, dtype=f32)
    shared = {
        "wbase_t": np.ascontiguousarray(np.asarray(L0_ffn_W_base, f32).T),
        "router_t": tilemaj(np.asarray(L0_ffn_router_W, f32).T),
        "a2_t": tilemaj(
            np.transpose(np.asarray(L0_ffn_A, f32), (2, 0, 1)).reshape(D, 128)),
        "bw2_t": np.ascontiguousarray(
            np.transpose(np.asarray(L0_ffn_Bw, f32), (0, 2, 1)).reshape(128, D)),
        "convw": tilemaj(np.asarray(L0_attn_conv_w, f32)[:, 0, :]),
        "convb": tilemaj(np.asarray(L0_attn_conv_b, f32)[:, None]),
        "xproj_t": tilemaj(np.asarray(L0_attn_xproj_w, f32).T),
        "dtproj_t": np.ascontiguousarray(np.asarray(L0_attn_dtproj_w, f32).T),
        "dtprojb": tilemaj(np.asarray(L0_attn_dtproj_b, f32)[:, None]),
        "alog": tilemaj(np.asarray(L0_attn_A_log, f32)),
        "dskip": tilemaj(np.asarray(L0_attn_D_skip, f32)[:, None]),
        "bbase": tilemaj(np.asarray(L0_ffn_b_base, f32)[:, None]),
        "repmat": np.ascontiguousarray(
            np.repeat(np.eye(NEXP, dtype=f32) * 2.0, R, axis=1)),
        "ident": np.eye(128, dtype=f32),
        "consts": np.tile(np.array([LNEPS, 1.0, -1.0, -0.5, 0.5], f32), (128, 1)),
    }
    in_maps = []
    for c in range(NCORES):
        bb = c // 4
        t0 = (c % 4) * T
        prevhot = np.zeros((128, NCORES), f32)
        if c % 4 != 0:
            prevhot[:, c - 1] = 1.0
        cmb = np.tile(np.array([0, 1, 1, 1, 0, 1, 1, 1], f32), (128, 1))
        cm8 = np.tile(np.tile(np.array([0, 1, 1, 1, 0, 1, 1, 1], f32), 8), (128, 1))
        prev8 = np.tile(np.tile(prevhot[0], 8), (128, 1)).astype(f32)
        m = dict(shared)
        m["x"] = np.ascontiguousarray(x[bb, t0:t0 + T, :].T)
        m["prevhot"] = prevhot
        m["cmb"] = cmb
        m["cm8"] = cm8
        m["prev8"] = prev8
        m["prev24"] = np.tile(np.repeat(prevhot[0], 3), (128, 1)).astype(f32)
        in_maps.append(m)

    res = run_bass_kernel_spmd(nc, in_maps, core_ids=list(range(NCORES)))
    _CACHE["res"] = res
    out = np.empty((B, S, D), dtype=f32)
    for c in range(NCORES):
        bb = c // 4
        t0 = (c % 4) * T
        out[bb, t0:t0 + T, :] = res.results[c]["out"].T
    return out


# version 34
# speedup vs baseline: 1.0053x; 1.0053x over previous
import contextlib
import numpy as np

import concourse.bass as bass
import concourse.mybir as mybir
from concourse.bass_utils import run_bass_kernel_spmd

F32 = mybir.dt.float32
BF16 = mybir.dt.bfloat16
AF = mybir.ActivationFunctionType
ALU = mybir.AluOpType

B, S, D = 2, 4096, 2048
NCORES = 8
T = (B * S) // NCORES
NT = D // 128
NSTATE = 16
NH = 8
NEXP, R = 8, 16
LNEPS = 1e-5

SAME_ENGINE_SYNC = True
DEBUG_SPILL = False



class Prog:
    DMAQ = {"sy": ("dsy", "dsy1", "dsy2"), "p": ("dgp", "dgp1", "dgp2"),
            "sc": ("dsc", "dsc1", "dsc2")}

    def __init__(self):
        self.ops = {e: [] for e in ("te", "v", "s", "p", "sy")}
        self.cnt = {}
        self.sat = {e: {} for e in self.ops}
        self.bufs = {}
        self.ccnames = []
        self.qrr = {}

    def buf(self, name, space, lo, hi):
        if name not in self.bufs:
            self.bufs[name] = dict(space=space, lo=lo, hi=hi, w=None, r=[])
        return name

    def _overlaps(self, name):
        b = self.bufs[name]
        return [n2 for n2, b2 in self.bufs.items()
                if b2["space"] == b["space"] and b2["lo"] < b["hi"] and b["lo"] < b2["hi"]]

    def _need(self, eng, ev, waits):
        if ev is None:
            return
        sem, val = ev
        if sem == eng:
            if not SAME_ENGINE_SYNC or val > self.cnt.get(eng, 0):
                return
        if self.sat[eng].get(sem, 0) >= val:
            return
        waits[sem] = max(waits.get(sem, 0), val)

    def _emit(self, eng, fn, reads, writes, inc_sem, inc_val, no_inc=False):
        waits = {}
        for b in reads:
            self._need(eng, self.bufs[b]["w"], waits)
        for b in writes:
            for ov in self._overlaps(b):
                self._need(eng, self.bufs[ov]["w"], waits)
                for ev in self.bufs[ov]["r"]:
                    self._need(eng, ev, waits)
        for sem, val in waits.items():
            self.sat[eng][sem] = max(self.sat[eng].get(sem, 0), val)
        if no_inc:
            ev = (inc_sem, self.cnt.get(inc_sem, 0) + inc_val)
            self.ops[eng].append((sorted(waits.items()), fn, (inc_sem, 0)))
            for b in writes:
                for ov in self._overlaps(b):
                    if ov != b:
                        self.bufs[ov]["w"] = None
                        self.bufs[ov]["r"] = []
                self.bufs[b]["w"] = ev
                self.bufs[b]["r"] = []
            for b in reads:
                self.bufs[b]["r"].append(ev)
            return ev
        self.cnt[inc_sem] = self.cnt.get(inc_sem, 0) + inc_val
        ev = (inc_sem, self.cnt[inc_sem])
        self.ops[eng].append((sorted(waits.items()), fn, (inc_sem, inc_val)))
        for b in writes:
            for ov in self._overlaps(b):
                if ov != b:
                    self.bufs[ov]["w"] = None
                    self.bufs[ov]["r"] = []
            self.bufs[b]["w"] = ev
            self.bufs[b]["r"] = []
        for b in reads:
            self.bufs[b]["r"].append(ev)
        return ev

    def op(self, eng, fn, reads=(), writes=(), inc=True):
        return self._emit(eng, fn, reads, writes, eng, 1, no_inc=not inc)

    ISSUER = {"sy": "sy", "p": "p", "sc": "s"}

    def dma(self, queue, fn, reads=(), writes=()):
        subs = self.DMAQ[queue]
        idx = self.qrr.get(queue, 0)
        self.qrr[queue] = (idx + 1) % len(subs)
        qsem = subs[idx]
        eng = self.ISSUER[queue]
        prev = self.cnt.get(qsem, 0)
        ev = self._emit(eng, fn, reads, writes, qsem, 16)
        if prev > 0:
            waits, fn2, inc = self.ops[eng][-1]
            if (qsem, prev) not in waits:
                waits = sorted(set(waits) | {(qsem, prev)})
                self.ops[eng][-1] = (waits, fn2, inc)
        return ev

    def cc(self, fn, reads=(), writes=()):
        name = f"cc{len(self.ccnames)}"
        self.ccnames.append(name)
        return self._emit("p", fn, reads, writes, name, 1)

    def wait_only(self, eng, sems):
        self.ops[eng].append((list(sems), lambda b: None, (None, 0)))

    def emit(self, nc, stack):
        sems = {}
        for s in sorted(set(self.cnt) | set(self.ccnames)):
            sems[s] = stack.enter_context(nc.semaphore(s))
        block = stack.enter_context(nc.Block())

        def section(builder, eng):
            for waits, fn, (isem, ival) in self.ops[eng]:
                for s, v in waits:
                    builder.wait_ge(sems[s], v)
                inst = fn(builder)
                if inst is not None and isem is not None and ival:
                    if isem.startswith("cc"):
                        inst.then_inc(sems[isem])
                    else:
                        inst.then_inc(sems[isem], ival)

        @block.tensor
        def _(b):
            section(b, "te")

        @block.vector
        def _(b):
            section(b, "v")

        @block.scalar
        def _(b):
            section(b, "s")

        @block.gpsimd
        def _(b):
            section(b, "p")

        @block.sync
        def _(b):
            section(b, "sy")


A0 = 0
B0 = 16384
C0 = 32768
SM = 40960

RING = [A0 + 1024 * r for r in range(12)]
E0P = (A0 + 12288, A0 + 13312)
OFF_DB = A0 + 14336

OFF_Y = C0
OFF_X1L = C0 + 1024
OFF_GATH = C0 + 2048
OFF_SUMM = C0 + 2176
OFF_TP = C0 + 2192
OFF_TN = C0 + 2200
OFF_HIN = C0 + 2208

OFF_RB = SM
OFF_RMB = SM + 1024
OFF_TMPB = SM + 2048
OFF_A2BF = SM + 3072
OFF_BW2BF = SM + 4096
OFF_RTRBF = SM + 5120
OFF_ASB = SM + 5184
OFF_CONVW = SM + 5440
OFF_CONVB = SM + 5504
OFF_DTPB = SM + 5520
OFF_DSKIP = SM + 5536
OFF_HALFB = SM + 5552
OFF_REP = SM + 5568
OFF_PREV = SM + 5696
OFF_CMB = SM + 5704
OFF_XBF = SM + 5712
OFF_XSQ = SM + 6224
OFF_LT = SM + 6736
OFF_MX = SM + 6768
OFF_MASKT = SM + 6772
OFF_HMBF = SM + 6804
OFF_T1A = SM + 7060
OFF_HALL = OFF_T1A
OFF_T1B = OFF_T1A
OFF_IDENT = SM + 7572
OFF_ONESBF = SM + 7700
OFF_CONSTS = SM + 7701
OFF_ONESC = SM + 7706
OFF_CM8 = SM + 7707
OFF_PREV8 = SM + 7771
OFF_TCH = SM + 7835
OFF_TSC = SM + 7899
OFF_PREV24 = SM + 7963
SM_END = SM + 7987
OFF_DELTA2 = (SM + 0, SM + 1024)
OFF_DU = SM + 2048
OFF_SDELTA = SM + 5712

OFF_XPJBF = C0
OFF_DTPJBF = C0 + 1280
OFF_DTBF = C0 + 2304
OFF_XN2 = C0 + 2816
OFF_UACC = C0 + 3840
OFF_TAILS = C0 + 4864
OFF_HX = C0 + 4912
OFF_HALO8 = C0 + 4920
OFF_WSTG = (C0 + 4096, C0 + 4608)
OFF_UT2 = (C0 + 2304, C0 + 3328)
OFF_WSTG2 = OFF_UT2
OFF_RT3 = C0 + 4944
OFF_MRT3 = C0 + 4947
TOTAL_WORDS = SM_END
assert TOTAL_WORDS <= 49100, TOTAL_WORDS


def build_program():
    nc = bass.Bass("TRN2", target_bir_lowering=False, debug=False)
    p = Prog()

    def din(name, shape, dtype=F32):
        ap = nc.dram_tensor(name, shape, dtype, kind="ExternalInput").ap()
        p.buf(name, "d:" + name, 0, 1)
        return ap

    x_h = din("x", [D, T])
    wb_h = din("wbase_t", [D, D])
    rtr_h = din("router_t", [128, 128])
    a2_h = din("a2_t", [128, 2048])
    bw2_h = din("bw2_t", [128, D])
    cw_h = din("convw", [128, 64])
    cb_h = din("convb", [128, 16])
    xp_h = din("xproj_t", [128, 2560])
    dtp_h = din("dtproj_t", [128, D])
    dtb_h = din("dtprojb", [128, 16])
    alog_h = din("alog", [128, 256])
    dsk_h = din("dskip", [128, 16])
    bb_h = din("bbase", [128, 16])
    rep_h = din("repmat", [NEXP, 128])
    prev_h = din("prevhot", [128, NCORES])
    cmb_h = din("cmb", [128, NCORES])
    cm8_h = din("cm8", [128, 8 * NCORES])
    prev8_h = din("prev8", [128, 8 * NCORES])
    prev24_h = din("prev24", [128, 24])
    id_h = din("ident", [128, 128])
    consts_h = din("consts", [128, 5])

    out_h = nc.dram_tensor("out", [D, T], F32, kind="ExternalOutput").ap()
    p.buf("out", "d:out", 0, 1)

    def dint(name, shape, dtype=F32, **kw):
        kind = "Internal"
        if DEBUG_SPILL and name in ("x1sp", "usp", "dsp", "ysp", "x2sp"):
            kind = "ExternalOutput"
        ap = nc.dram_tensor(name, shape, dtype, kind=kind, **kw).ap()
        p.buf(name, "d:" + name, 0, 1)
        return ap

    x1sp = dint("x1sp", [D, T])
    usp = dint("usp", [D, T])
    dsp = dint("dsp", [D, T])
    ysp = dint("ysp", [D, T])
    x2sp = dint("x2sp", [D, T])
    sdsp = dint("sdsp", [D, T])
    wbsp = dint("wbsp", [D, 2048], BF16)
    for _k in range(NT):
        for _nm in ("x1sp", "usp", "dsp", "ysp", "x2sp", "sdsp", "wbsp"):
            p.buf(f"{_nm}{_k}", "d:" + _nm, _k, _k + 1)
        p.bufs["x1sp"]["lo"], p.bufs["x1sp"]["hi"] = 0, NT
        p.bufs["usp"]["lo"], p.bufs["usp"]["hi"] = 0, NT
        p.bufs["dsp"]["lo"], p.bufs["dsp"]["hi"] = 0, NT
        p.bufs["ysp"]["lo"], p.bufs["ysp"]["hi"] = 0, NT
        p.bufs["x2sp"]["lo"], p.bufs["x2sp"]["hi"] = 0, NT
    bcB = dint("bcB", [NSTATE, T])
    bcC = dint("bcC", [NSTATE, T])
    stdr = dint("stdr", [2, T])
    stdt = dint("stdt", [2, 4])
    halo_in = dint("halo_in", [D, 3])
    halo_out = dint("halo_out", [NCORES * D, 3], addr_space="Shared")
    ccin = [dint(f"ccin{r}", [128, 2 * NH]) for r in range(32)]
    ccout = [dint(f"ccout{r}", [NCORES * 128, 2 * NH], addr_space="Shared")
             for r in range(32)]

    stack = contextlib.ExitStack()
    arena = stack.enter_context(nc.sbuf_tensor("arena", [128, TOTAL_WORDS], F32)).ap()
    pall = stack.enter_context(nc.psum_tensor("pall", [128, 4096], F32)).ap()
    row_m = stack.enter_context(nc.sbuf_tensor("row_m", [1, T], F32)).ap()
    row_e = stack.enter_context(nc.sbuf_tensor("row_e", [1, T], F32)).ap()
    row_l = stack.enter_context(nc.sbuf_tensor("row_l", [NEXP, 512], F32)).ap()
    row_msk = stack.enter_context(nc.sbuf_tensor("row_msk", [NEXP, 512], F32)).ap()
    row_bc = stack.enter_context(nc.sbuf_tensor("row_bc", [NSTATE, T], F32)).ap()
    for nm in ("row_m", "row_e", "row_l", "row_msk", "row_bc"):
        p.buf(nm, nm, 0, 1)

    def sb(off, words):
        return arena[:, off:off + words]

    def sbbf(off, fpwords):
        return arena[:, off:off + fpwords].bitcast(BF16)

    def reg(name, off, words):
        return p.buf(name, "sbuf", off, off + words)

    def regp(name, bank, nbanks=1):
        return p.buf(name, "psum", bank * 512, (bank + nbanks) * 512)

    def cst(i):
        return arena[:, OFF_CONSTS + i:OFF_CONSTS + i + 1]
    C_EPS, C_ONE, C_NEG1, C_NEGH, C_HALF = 0, 1, 2, 3, 4

    for k in range(NT):
        reg(f"xres{k}", A0 + 1024 * k, 1024)
    for qq in range(8):
        p.dma("sy" if qq % 2 == 0 else "sc", lambda b, qq=qq: b.dma_start(
            out=sb(A0 + 2048 * qq, 2048).rearrange("p (k t) -> p k t", k=2),
            in_=x_h.rearrange("(k p) t -> p k t", p=128)[:, 2 * qq:2 * (qq + 1), :]),
            reads=["x"], writes=[f"xres{2 * qq + k2}" for k2 in range(2)])

    reg("convw", OFF_CONVW, 64)
    p.dma("sy", lambda b: b.dma_start(out=sb(OFF_CONVW, 64), in_=cw_h[:]),
          reads=["convw"], writes=["convw"])
    reg("convb", OFF_CONVB, 16)
    p.dma("sy", lambda b: b.dma_start(out=sb(OFF_CONVB, 16), in_=cb_h[:]),
          reads=["convb"], writes=["convb"])
    reg("dtpb", OFF_DTPB, 16)
    p.dma("sy", lambda b: b.dma_start(out=sb(OFF_DTPB, 16), in_=dtb_h[:]),
          reads=["dtprojb"], writes=["dtpb"])
    reg("dskip", OFF_DSKIP, 16)
    p.dma("sy", lambda b: b.dma_start(out=sb(OFF_DSKIP, 16), in_=dsk_h[:]),
          reads=["dskip"], writes=["dskip"])
    reg("bbase", OFF_HALFB, 16)
    p.dma("sy", lambda b: b.dma_start(out=sb(OFF_HALFB, 16), in_=bb_h[:]),
          reads=["bbase"], writes=["bbase"])
    reg("tmpbc", OFF_TMPB, 1024)
    p.dma("sy", lambda b: b.dma_start(out=sb(OFF_TMPB, 256), in_=alog_h[:]),
          reads=["alog"], writes=["tmpbc"])
    reg("repm", OFF_REP, 128)
    p.dma("sy", lambda b: b.dma_start(
        out=arena[0:NEXP, OFF_REP:OFF_REP + 128], in_=rep_h[:]),
        reads=["repmat"], writes=["repm"])
    reg("prevb", OFF_PREV, 8)
    p.dma("sy", lambda b: b.dma_start(out=sb(OFF_PREV, 8), in_=prev_h[:]),
          reads=["prevhot"], writes=["prevb"])
    reg("cmbb", OFF_CMB, 8)
    p.dma("sy", lambda b: b.dma_start(out=sb(OFF_CMB, 8), in_=cmb_h[:]),
          reads=["cmb"], writes=["cmbb"])
    reg("cm8", OFF_CM8, 64)
    p.dma("sy", lambda b: b.dma_start(out=sb(OFF_CM8, 64), in_=cm8_h[:]),
          reads=["cm8"], writes=["cm8"])
    reg("prev8", OFF_PREV8, 64)
    p.dma("sy", lambda b: b.dma_start(out=sb(OFF_PREV8, 64), in_=prev8_h[:]),
          reads=["prev8"], writes=["prev8"])
    reg("prev24", OFF_PREV24, 24)
    p.dma("sy", lambda b: b.dma_start(out=sb(OFF_PREV24, 24), in_=prev24_h[:]),
          reads=["prev24"], writes=["prev24"])
    reg("ident", OFF_IDENT, 128)
    p.dma("sy", lambda b: b.dma_start(out=sb(OFF_IDENT, 128), in_=id_h[:]),
          reads=["ident"], writes=["ident"])
    reg("consts", OFF_CONSTS, 5)
    p.dma("sy", lambda b: b.dma_start(out=sb(OFF_CONSTS, 5), in_=consts_h[:]),
          reads=["consts"], writes=["consts"])

    p.op("v", lambda b: b.tensor_scalar(
        sb(OFF_HALFB, 16), sb(OFF_HALFB, 16), 0.5, None, ALU.mult),
        reads=["bbase"], writes=["bbase"])

    reg("Asb", OFF_ASB, 256)
    p.op("s", lambda b: b.activation(sb(OFF_ASB, 256), sb(OFF_TMPB, 256), AF.Exp),
         reads=["tmpbc"], writes=["Asb"])
    p.op("v", lambda b: b.tensor_scalar(
        sb(OFF_ASB, 256), sb(OFF_ASB, 256), -1.0, None, ALU.mult),
        reads=["Asb"], writes=["Asb"])

    reg("onesbf", OFF_ONESBF, 1)
    p.op("v", lambda b: b.memset(sbbf(OFF_ONESBF, 1), 1.0), writes=["onesbf"])
    reg("onesc", OFF_ONESC, 1)
    p.op("v", lambda b: b.memset(sb(OFF_ONESC, 1), 1.0), writes=["onesc"])

    STG0, STG1 = C0, C0 + 2048
    reg("stg0", STG0, 2048)
    reg("stg1", STG1, 2048)

    def load_cast(dst_off, n_elems, src_ap, src_buf, dst_buf, stg_off, stg_buf,
                  q="sy"):
        p.dma(q, lambda b: b.dma_start(out=sb(stg_off, n_elems), in_=src_ap),
              reads=[src_buf], writes=[stg_buf])
        p.op("v", lambda b: b.tensor_copy(
            sbbf(dst_off, (n_elems + 1) // 2)[:, 0:n_elems],
            sb(stg_off, n_elems)),
            reads=[stg_buf], writes=[dst_buf])

    for k in range(NT):
        reg(f"wbf{k}", B0 + 1024 * k, 1024)
        load_cast(B0 + 1024 * k, 2048, wb_h[128 * k:128 * (k + 1), :],
                  "wbase_t", f"wbf{k}", STG0 if k % 2 == 0 else STG1,
                  "stg0" if k % 2 == 0 else "stg1", q="p")
        p.dma("sc", lambda b, k=k: b.dma_start(
            out=wbsp[128 * k:128 * (k + 1), :],
            in_=sbbf(B0 + 1024 * k, 1024)[:, 0:2048]),
            reads=[f"wbf{k}"], writes=[f"wbsp{k}"])

    reg("tmpbc", OFF_TMPB, 1024)
    reg("a2bf", OFF_A2BF, 1024)
    p.dma("sy", lambda b: b.dma_start(out=sb(STG0, 2048), in_=a2_h[:]),
          reads=["a2_t"], writes=["stg0"])
    p.op("s", lambda b: b.activation(
        sbbf(OFF_A2BF, 1024)[:, 0:2048], sb(STG0, 2048), AF.Copy),
        reads=["stg0"], writes=["a2bf"])

    reg("bw2bf", OFF_BW2BF, 1024)
    p.dma("sy", lambda b: b.dma_start(out=sb(STG1, 2048), in_=bw2_h[:]),
          reads=["bw2_t"], writes=["stg1"])
    p.op("s", lambda b: b.activation(
        sbbf(OFF_BW2BF, 1024)[:, 0:2048], sb(STG1, 2048), AF.Copy),
        reads=["stg1"], writes=["bw2bf"])

    reg("rtrbf", OFF_RTRBF, 64)
    p.dma("sy", lambda b: b.dma_start(out=sb(OFF_TMPB, 128), in_=rtr_h[:]),
          reads=["router_t"], writes=["tmpbc"])
    p.op("s", lambda b: b.activation(
        sbbf(OFF_RTRBF, 64)[:, 0:128], sb(OFF_TMPB, 128), AF.Copy),
        reads=["tmpbc"], writes=["rtrbf"])

    reg("rb", OFF_RB, 1024)
    reg("rmb", OFF_RMB, 1024)
    reg("xbf", OFF_XBF, 512)
    reg("xsq", OFF_XSQ, 512)

    def emit_ln(x_off_fn, x_buf_fn, tag, skip_stats=False):
        sx = regp(f"ps_sx_{tag}", 0, 2)
        sxx = regp(f"ps_sxx_{tag}", 2, 2)
        for k in range(NT if not skip_stats else 0):
            xo = OFF_XBF if k % 2 == 0 else OFF_RB
            so = OFF_XSQ if k % 2 == 0 else OFF_RMB
            xb = "xbf" if k % 2 == 0 else "rb"
            sq = "xsq" if k % 2 == 0 else "rmb"
            p.op("s", lambda b, k=k, xo=xo: b.activation(
                sbbf(xo, 512)[:, 0:1024], sb(x_off_fn(k), 1024), AF.Copy),
                reads=[x_buf_fn(k)], writes=[xb])
            p.op("s", lambda b, xo=xo, so=so: b.activation(
                sbbf(so, 512)[:, 0:1024], sbbf(xo, 512)[:, 0:1024],
                AF.Square),
                reads=[xb], writes=[sq])
            for tc in range(2):
                p.op("te", lambda b, k=k, tc=tc, xo=xo: b.matmul(
                    pall[0:1, tc * 512:(tc + 1) * 512],
                    sbbf(OFF_ONESBF, 1)[:, 0:1],
                    sbbf(xo, 512)[:, tc * 512:(tc + 1) * 512],
                    start=(k == 0), stop=(k == NT - 1)),
                    reads=[xb, "onesbf"], writes=[sx], inc=False)
                p.op("te", lambda b, k=k, tc=tc, so=so: b.matmul(
                    pall[0:1, 1024 + tc * 512:1024 + (tc + 1) * 512],
                    sbbf(OFF_ONESBF, 1)[:, 0:1],
                    sbbf(so, 512)[:, tc * 512:(tc + 1) * 512],
                    start=(k == 0), stop=(k == NT - 1)),
                    reads=[sq, "onesbf"], writes=[sxx], inc=(tc == 1))
        p.op("v", lambda b: b.tensor_scalar(
            row_m[:], pall[0:1, 0:1024], 1.0 / D, None, ALU.mult),
            reads=[sx], writes=["row_m"])
        p.op("v", lambda b: b.tensor_scalar(
            row_e[:], pall[0:1, 1024:2048], 1.0 / D, None, ALU.mult),
            reads=[sxx], writes=["row_e"])
        p.dma("sy", lambda b: b.dma_start(out=stdr[0:1, :], in_=row_m[:]),
              reads=["row_m"], writes=["stdr"])
        p.dma("sy", lambda b: b.dma_start(out=stdr[1:2, :], in_=row_e[:]),
              reads=["row_e"], writes=["stdr"])
        p.dma("sy", lambda b: b.dma_start(
            out=sb(OFF_TMPB, 1024), in_=stdr[0:1, :].partition_broadcast(128)),
            reads=["stdr"], writes=["tmpbc"])
        p.dma("sy", lambda b: b.dma_start(
            out=sb(OFF_RB, 1024), in_=stdr[1:2, :].partition_broadcast(128)),
            reads=["stdr"], writes=["rb"])
        p.op("v", lambda b: b.tensor_tensor(
            sb(OFF_RMB, 1024), sb(OFF_TMPB, 1024), sb(OFF_TMPB, 1024), ALU.mult),
            reads=["tmpbc"], writes=["rmb"])
        p.op("v", lambda b: b.tensor_tensor(
            sb(OFF_RB, 1024), sb(OFF_RB, 1024), sb(OFF_RMB, 1024), ALU.subtract),
            reads=["rb", "rmb"], writes=["rb"])
        p.op("s", lambda b: b.activation(
            sb(OFF_RB, 1024), sb(OFF_RB, 1024), AF.Ln, bias=cst(C_EPS)),
            reads=["rb", "consts"], writes=["rb"])
        p.op("s", lambda b: b.activation(
            sb(OFF_RB, 1024), sb(OFF_RB, 1024), AF.Exp, scale=cst(C_NEGH)),
            reads=["rb", "consts"], writes=["rb"])
        p.op("v", lambda b: b.tensor_tensor(
            sb(OFF_RMB, 1024), sb(OFF_TMPB, 1024), sb(OFF_RB, 1024), ALU.mult),
            reads=["tmpbc", "rb"], writes=["rmb"])

    reg("lt", OFF_LT, 32)
    reg("mx", OFF_MX, 4)
    reg("maskT", OFF_MASKT, 32)
    reg("hall", OFF_HALL, 512)
    reg("hmbf", OFF_HMBF, 256)
    reg("t1a", OFF_T1A, 512)
    reg("sdelta", OFF_SDELTA, 1024)
    for _k in range(4):
        reg(f"xnp{_k}", C0 + 4096 + 256 * _k, 256)

    def emit_ffn(x_off_fn, x_buf_fn, xn_off, xn_buf_fn, out_write_fn, tag,
                 skip_stats=False):
        emit_ln(x_off_fn, x_buf_fn, tag, skip_stats=skip_stats)
        PRE = 4

        def xslot(k, tc):
            return (C0 + 4096 + 256 * k) if (tc == 1 and k < PRE) \
                else (xn_off + 256 * k)

        def xname(k, tc):
            return f"xnp{k}" if (tc == 1 and k < PRE) else xn_buf_fn(k)

        def xn_build(k, tc, dst, dbuf):
            p.op("v", lambda b, k=k, tc=tc: b.tensor_tensor(
                sb(OFF_TMPB, 512),
                sb(x_off_fn(k), 1024)[:, tc * 512:(tc + 1) * 512],
                sb(OFF_RB, 1024)[:, tc * 512:(tc + 1) * 512], ALU.mult),
                reads=[x_buf_fn(k), "rb"], writes=["tmpbc"])
            p.op("v", lambda b, tc=tc, dst=dst: b.tensor_tensor(
                sbbf(dst, 256)[:, 0:512], sb(OFF_TMPB, 512),
                sb(OFF_RMB, 1024)[:, tc * 512:(tc + 1) * 512], ALU.subtract),
                reads=["tmpbc", "rmb"], writes=[dbuf])

        for tc in range(2):
            for k in range(PRE if tc == 1 else 0, NT):
                xn_build(k, tc, xn_off + 256 * k, xn_buf_fn(k))
            rps = regp("ps_rtr", 0, 1)
            for k in range(NT):
                p.op("te", lambda b, k=k, tc=tc, xs=xslot(k, tc): b.matmul(
                    pall[0:NEXP, 0:512],
                    sbbf(OFF_RTRBF, 64)[:, 8 * k:8 * (k + 1)],
                    sbbf(xs, 256)[:, 0:512],
                    start=(k == 0), stop=(k == NT - 1)),
                    reads=[xname(k, tc), "rtrbf"], writes=[rps],
                    inc=(k == NT - 1))
            p.op("s", lambda b: b.copy(row_l[:], pall[0:NEXP, 0:512]),
                 reads=[rps], writes=["row_l"])
            tps = regp("ps_tr", 1, 1)
            for j in range(4):
                p.op("te", lambda b, j=j: b.transpose(
                    pall[:, 512:520], row_l[:, 128 * j:128 * (j + 1)],
                    arena[0:NEXP, OFF_IDENT:OFF_IDENT + NEXP]),
                    reads=["row_l", "ident"], writes=[tps])
                p.op("s", lambda b, j=j: b.copy(
                    arena[:, OFF_LT + 8 * j:OFF_LT + 8 * (j + 1)],
                    pall[:, 512:520]),
                    reads=[tps], writes=["lt"])
            p.op("v", lambda b: b.tensor_reduce(
                sb(OFF_MX, 4), sb(OFF_LT, 32).rearrange("p (j e) -> p j e", j=4),
                mybir.AxisListType.X, ALU.max),
                reads=["lt"], writes=["mx"])
            for j in range(4):
                p.op("v", lambda b, j=j: b.tensor_scalar(
                    arena[:, OFF_MASKT + 8 * j:OFF_MASKT + 8 * (j + 1)],
                    arena[:, OFF_LT + 8 * j:OFF_LT + 8 * (j + 1)],
                    arena[:, OFF_MX + j:OFF_MX + j + 1], None, ALU.is_ge),
                    reads=["lt", "mx"], writes=["maskT"])
            for j in range(4):
                p.op("te", lambda b, j=j: b.transpose(
                    pall[0:NEXP, 512:640],
                    arena[:, OFF_MASKT + 8 * j:OFF_MASKT + 8 * (j + 1)],
                    sb(OFF_IDENT, 128)),
                    reads=["maskT", "ident"], writes=[tps])
                p.op("s", lambda b, j=j: b.copy(
                    row_msk[:, 128 * j:128 * (j + 1)], pall[0:NEXP, 512:640]),
                    reads=[tps], writes=["row_msk"])
            mps = regp("ps_mer", 2, 1)
            p.op("te", lambda b: b.matmul(
                pall[:, 1024:1536], arena[0:NEXP, OFF_REP:OFF_REP + 128],
                row_msk[:], start=True, stop=True),
                reads=["row_msk", "repm"], writes=[mps])
            hps = regp("ps_hall", 3, 1)
            for k in range(NT):
                p.op("te", lambda b, k=k, tc=tc, xs=xslot(k, tc): b.matmul(
                    pall[:, 1536:2048],
                    sbbf(OFF_A2BF, 1024)[:, 128 * k:128 * (k + 1)],
                    sbbf(xs, 256)[:, 0:512],
                    start=(k == 0), stop=(k == NT - 1)),
                    reads=[xname(k, tc), "a2bf"], writes=[hps],
                    inc=(k == NT - 1))
            p.op("s", lambda b: b.copy(sb(OFF_HALL, 512), pall[:, 1536:2048]),
                 reads=[hps], writes=["hall"])
            p.op("v", lambda b: b.tensor_tensor(
                sbbf(OFF_HMBF, 256)[:, 0:512], sb(OFF_HALL, 512),
                pall[:, 1024:1536], ALU.mult),
                reads=["hall", mps], writes=["hmbf"])
            if tc == 0:
                for k in range(PRE):
                    xn_build(k, 1, C0 + 4096 + 256 * k, f"xnp{k}")
            for o in range(NT):
                bank = 4 + (o % 2)
                bps = regp(f"ps_base{o % 2}", bank, 1)
                t1off = OFF_T1A if o % 2 == 0 else OFF_SDELTA
                t1buf = "t1a" if o % 2 == 0 else "sdelta"
                for k in range(NT):
                    p.op("te", lambda b, o=o, k=k, bank=bank,
                         xs=xslot(k, tc): b.matmul(
                        pall[:, bank * 512:(bank + 1) * 512],
                        sbbf(B0 + 1024 * k, 1024)[:, 128 * o:128 * (o + 1)],
                        sbbf(xs, 256)[:, 0:512],
                        start=(k == 0), stop=False),
                        reads=[xname(k, tc), f"wbf{k}"], writes=[bps],
                        inc=False)
                p.op("te", lambda b, o=o, bank=bank: b.matmul(
                    pall[:, bank * 512:(bank + 1) * 512],
                    sbbf(OFF_BW2BF, 1024)[:, 128 * o:128 * (o + 1)],
                    sbbf(OFF_HMBF, 256)[:, 0:512], start=False, stop=True),
                    reads=["hmbf", "bw2bf"], writes=[bps])
                p.op("s", lambda b, o=o, bank=bank, t1off=t1off: b.activation(
                    sb(t1off, 512), pall[:, bank * 512:(bank + 1) * 512],
                    AF.Identity, scale=cst(C_HALF),
                    bias=arena[:, OFF_HALFB + o:OFF_HALFB + o + 1]),
                    reads=[bps, "bbase", "consts"], writes=[t1buf])
                out_write_fn(o, tc, t1off, t1buf)

    def p1_out(o, tc, t1off, t1buf):
        p.op("v", lambda b: b.tensor_tensor(
            arena[:, A0 + 1024 * o + 512 * tc:A0 + 1024 * o + 512 * (tc + 1)],
            arena[:, A0 + 1024 * o + 512 * tc:A0 + 1024 * o + 512 * (tc + 1)],
            sb(t1off, 512), ALU.add),
            reads=[t1buf, f"xres{o}"], writes=[f"xres{o}"])

    for k in range(NT):
        reg(f"xn1bf{k}", C0 + 256 * k, 256)
    emit_ffn(lambda k: A0 + 1024 * k, lambda k: f"xres{k}",
             C0, lambda k: f"xn1bf{k}", p1_out, "ln1")

    for k in range(NT):
        p.dma("sy", lambda b, k=k: b.dma_start(
            out=x1sp[128 * k:128 * (k + 1), :], in_=sb(A0 + 1024 * k, 1024)),
            reads=[f"xres{k}"], writes=[f"x1sp{k}"])

    sxt = regp("ps_sxt", 6, 1)
    sxxt = regp("ps_sxxt", 7, 1)
    for q in range(2):
        reg(f"tl{q}", OFF_XBF + 2 * q, 2)
        reg(f"ts{q}", OFF_XSQ + 2 * q, 2)
    for k in range(NT):
        xo, so = OFF_XBF + 2 * (k % 2), OFF_XSQ + 2 * (k % 2)
        xb, sq = f"tl{k % 2}", f"ts{k % 2}"
        p.op("s", lambda b, k=k, xo=xo: b.activation(
            sbbf(xo, 2)[:, 0:3],
            arena[:, A0 + 1024 * k + 1021:A0 + 1024 * k + 1024], AF.Copy),
            reads=[f"xres{k}"], writes=[xb])
        p.op("s", lambda b, xo=xo, so=so: b.activation(
            sbbf(so, 2)[:, 0:3], sbbf(xo, 2)[:, 0:3], AF.Square),
            reads=[xb], writes=[sq])
        p.op("te", lambda b, k=k, xo=xo: b.matmul(
            pall[0:1, 3072:3075], sbbf(OFF_ONESBF, 1)[:, 0:1],
            sbbf(xo, 2)[:, 0:3], start=(k == 0), stop=(k == NT - 1)),
            reads=[xb, "onesbf"], writes=[sxt])
        p.op("te", lambda b, k=k, so=so: b.matmul(
            pall[0:1, 3584:3587], sbbf(OFF_ONESBF, 1)[:, 0:1],
            sbbf(so, 2)[:, 0:3], start=(k == 0), stop=(k == NT - 1)),
            reads=[sq, "onesbf"], writes=[sxxt])
    p.op("v", lambda b: b.tensor_scalar(
        row_m[:, 0:3], pall[0:1, 3072:3075], 1.0 / D, None, ALU.mult),
        reads=[sxt], writes=["row_m"])
    p.op("v", lambda b: b.tensor_scalar(
        row_e[:, 0:3], pall[0:1, 3584:3587], 1.0 / D, None, ALU.mult),
        reads=[sxxt], writes=["row_e"])
    p.op("v", lambda b: b.tensor_tensor(
        row_m[:, 4:7], row_m[:, 0:3], row_m[:, 0:3], ALU.mult),
        reads=["row_m"], writes=["row_m"])
    p.op("v", lambda b: b.tensor_tensor(
        row_e[:, 0:3], row_e[:, 0:3], row_m[:, 4:7], ALU.subtract),
        reads=["row_e", "row_m"], writes=["row_e"])
    p.op("s", lambda b: b.activation(
        row_e[:, 0:3], row_e[:, 0:3], AF.Ln, bias=cst(C_EPS)[0:1, :]),
        reads=["row_e", "consts"], writes=["row_e"])
    p.op("s", lambda b: b.activation(
        row_e[:, 0:3], row_e[:, 0:3], AF.Exp, scale=cst(C_NEGH)[0:1, :]),
        reads=["row_e", "consts"], writes=["row_e"])
    p.op("v", lambda b: b.tensor_tensor(
        row_m[:, 4:7], row_m[:, 0:3], row_e[:, 0:3], ALU.mult),
        reads=["row_m", "row_e"], writes=["row_m"])
    p.dma("sy", lambda b: b.dma_start(out=stdt[0:1, 0:3], in_=row_e[:, 0:3]),
          reads=["row_e"], writes=["stdt"])
    p.dma("sy", lambda b: b.dma_start(out=stdt[1:2, 0:3], in_=row_m[:, 4:7]),
          reads=["row_m"], writes=["stdt"])
    reg("rt3", OFF_RT3, 3)
    reg("mrt3", OFF_MRT3, 3)
    p.dma("sy", lambda b: b.dma_start(
        out=sb(OFF_RT3, 3), in_=stdt[0:1, 0:3].partition_broadcast(128)),
        reads=["stdt"], writes=["rt3"])
    p.dma("sy", lambda b: b.dma_start(
        out=sb(OFF_MRT3, 3), in_=stdt[1:2, 0:3].partition_broadcast(128)),
        reads=["stdt"], writes=["mrt3"])
    reg("tails", OFF_TAILS, 48)
    for k in range(NT):
        p.op("v", lambda b, k=k: b.tensor_tensor(
            arena[:, OFF_TAILS + 3 * k:OFF_TAILS + 3 * (k + 1)],
            arena[:, A0 + 1024 * k + 1021:A0 + 1024 * k + 1024],
            sb(OFF_RT3, 3), ALU.mult),
            reads=[f"xres{k}", "rt3"], writes=["tails"])
        p.op("v", lambda b, k=k: b.tensor_tensor(
            arena[:, OFF_TAILS + 3 * k:OFF_TAILS + 3 * (k + 1)],
            arena[:, OFF_TAILS + 3 * k:OFF_TAILS + 3 * (k + 1)],
            sb(OFF_MRT3, 3), ALU.subtract),
            reads=["tails", "mrt3"], writes=["tails"])
    p.dma("p", lambda b: b.dma_start(
        out=halo_in.rearrange("(k p) c -> p k c", p=128),
        in_=sb(OFF_TAILS, 48).rearrange("p (k c) -> p k c", k=NT)),
        reads=["tails"], writes=["halo_in"])
    p.cc(lambda b: b.collective_compute(
        "AllGather", ALU.bypass, replica_groups=[list(range(NCORES))],
        ins=[halo_in.opt()], outs=[halo_out.opt()]),
        reads=["halo_in"], writes=["halo_out"])

    emit_ln(lambda k: A0 + 1024 * k, lambda k: f"xres{k}", "ln2")

    reg("pstg", OFF_XN2, 2048)
    reg("xpjbf", OFF_XPJBF, 1280)
    for hh in range(2):
        p.dma("sy", lambda b, hh=hh: b.dma_start(
            out=sb(OFF_XN2, 1280), in_=xp_h[:, 1280 * hh:1280 * (hh + 1)]),
            reads=["xproj_t"], writes=["pstg"])
        p.op("s", lambda b, hh=hh: b.activation(
            sbbf(OFF_XPJBF + 640 * hh, 640)[:, 0:1280], sb(OFF_XN2, 1280), AF.Copy),
            reads=["pstg"], writes=["xpjbf"])
    reg("dtpjbf", OFF_DTPJBF, 1024)
    p.dma("sy", lambda b: b.dma_start(out=sb(OFF_XN2, 2048), in_=dtp_h[:]),
          reads=["dtproj_t"], writes=["pstg"])
    p.op("s", lambda b: b.activation(
        sbbf(OFF_DTPJBF, 1024)[:, 0:2048], sb(OFF_XN2, 2048), AF.Copy),
        reads=["pstg"], writes=["dtpjbf"])


    dtps = regp("ps_dt", 0, 2)
    bps_ = regp("ps_B", 2, 2)
    cps_ = regp("ps_C", 4, 2)
    reg("xn2", OFF_XN2, 1024)
    reg("uacc", OFF_UACC, 1024)
    reg("halo8", OFF_HALO8, 24)
    reg("hx", OFF_HX, 8)
    for k in range(NT):
        ua = A0 + 1024 * k
        ub = f"xres{k}"
        p.op("v", lambda b, k=k: b.tensor_tensor(
            sb(OFF_XN2, 1024), sb(A0 + 1024 * k, 1024), sb(OFF_RB, 1024),
            ALU.mult),
            reads=[f"xres{k}", "rb"], writes=["xn2"])
        p.op("v", lambda b: b.tensor_tensor(
            sb(OFF_XN2, 1024), sb(OFF_XN2, 1024), sb(OFF_RMB, 1024),
            ALU.subtract),
            reads=["xn2", "rmb"], writes=["xn2"])
        p.dma("p", lambda b, k=k: b.dma_start(
            out=sb(OFF_HALO8, 24).rearrange("p (j c) -> p j c", j=8),
            in_=halo_out.rearrange("(j k2 p) c -> p j k2 c", j=8, p=128)[:, :, k, :]),
            reads=["halo_out"], writes=["halo8"])
        p.op("v", lambda b, k=k, ua=ua: b.tensor_scalar(
            arena[:, ua + 3:ua + 1024],
            arena[:, OFF_XN2 + 3:OFF_XN2 + 1024],
            arena[:, OFF_CONVW + 4 * k + 3:OFF_CONVW + 4 * k + 4],
            arena[:, OFF_CONVB + k:OFF_CONVB + k + 1], ALU.mult, ALU.add),
            reads=["xn2", "convw", "convb"], writes=[ub])
        for j in (2, 1, 0):
            p.op("v", lambda b, k=k, j=j, ua=ua: b.scalar_tensor_tensor(
                arena[:, ua + 3:ua + 1024],
                arena[:, OFF_XN2 + j:OFF_XN2 + j + 1021],
                arena[:, OFF_CONVW + 4 * k + j:OFF_CONVW + 4 * k + j + 1],
                arena[:, ua + 3:ua + 1024], ALU.mult, ALU.add),
                reads=["xn2", "convw", ub], writes=[ub])
        p.op("v", lambda b: b.tensor_tensor(
            sb(OFF_HALO8, 24), sb(OFF_HALO8, 24),
            arena[:, OFF_PREV24:OFF_PREV24 + 24], ALU.mult),
            reads=["halo8", "prev24"], writes=["halo8"])
        p.op("v", lambda b: b.tensor_reduce(
            sb(OFF_HX, 3), sb(OFF_HALO8, 24).rearrange(
                "p (j c) -> p c j", c=3) if False else
            sb(OFF_HALO8, 24).rearrange("p (j c) -> p j c", j=8).transpose(
                [0, 2, 1]),
            mybir.AxisListType.X, ALU.add),
            reads=["halo8"], writes=["hx"])
        p.op("v", lambda b: b.tensor_copy(
            arena[:, OFF_HX + 3:OFF_HX + 6], sb(OFF_XN2, 3)),
            reads=["xn2"], writes=["hx"])
        p.op("v", lambda b, k=k, ua=ua: b.tensor_scalar(
            arena[:, ua:ua + 3], arena[:, OFF_HX + 3:OFF_HX + 6],
            arena[:, OFF_CONVW + 4 * k + 3:OFF_CONVW + 4 * k + 4],
            arena[:, OFF_CONVB + k:OFF_CONVB + k + 1], ALU.mult, ALU.add),
            reads=["hx", "convw", "convb"], writes=[ub])
        for j in (2, 1, 0):
            p.op("v", lambda b, k=k, j=j, ua=ua: b.scalar_tensor_tensor(
                arena[:, ua:ua + 3], arena[:, OFF_HX + j:OFF_HX + j + 3],
                arena[:, OFF_CONVW + 4 * k + j:OFF_CONVW + 4 * k + j + 1],
                arena[:, ua:ua + 3], ALU.mult, ALU.add),
                reads=["hx", "convw", ub], writes=[ub])
        p.dma("sy", lambda b, k=k, ua=ua: b.dma_start(
            out=usp[128 * k:128 * (k + 1), :], in_=sb(ua, 1024)),
            reads=[ub], writes=[f"usp{k}"])
        p.op("s", lambda b, ua=ua: b.activation(
            sbbf(OFF_XBF, 512)[:, 0:1024], sb(ua, 1024), AF.Copy),
            reads=[ub], writes=["xbf"])
        for tc in range(2):
            p.op("te", lambda b, k=k, tc=tc: b.matmul(
                pall[:, tc * 512:(tc + 1) * 512],
                sbbf(OFF_XPJBF, 1280)[:, 160 * k:160 * k + 128],
                sbbf(OFF_XBF, 512)[:, tc * 512:(tc + 1) * 512],
                start=(k == 0), stop=(k == NT - 1)),
                reads=["xbf", "xpjbf"], writes=[dtps], inc=False)
            p.op("te", lambda b, k=k, tc=tc: b.matmul(
                pall[0:NSTATE, 1024 + tc * 512:1024 + (tc + 1) * 512],
                sbbf(OFF_XPJBF, 1280)[:, 160 * k + 128:160 * k + 144],
                sbbf(OFF_XBF, 512)[:, tc * 512:(tc + 1) * 512],
                start=(k == 0), stop=(k == NT - 1)),
                reads=["xbf", "xpjbf"], writes=[bps_], inc=False)
            p.op("te", lambda b, k=k, tc=tc: b.matmul(
                pall[0:NSTATE, 2048 + tc * 512:2048 + (tc + 1) * 512],
                sbbf(OFF_XPJBF, 1280)[:, 160 * k + 144:160 * k + 160],
                sbbf(OFF_XBF, 512)[:, tc * 512:(tc + 1) * 512],
                start=(k == 0), stop=(k == NT - 1)),
                reads=["xbf", "xpjbf"], writes=[cps_], inc=(tc == 1))

    reg("dtbf", OFF_DTBF, 512)
    p.op("s", lambda b: b.activation(
        sbbf(OFF_DTBF, 512)[:, 0:1024], pall[:, 0:1024], AF.Copy),
        reads=[dtps], writes=["dtbf"])
    p.op("s", lambda b: b.copy(row_bc[:], pall[0:NSTATE, 1024:2048]),
         reads=[bps_], writes=["row_bc"])
    p.dma("sy", lambda b: b.dma_start(out=bcB[:], in_=row_bc[:]),
          reads=["row_bc"], writes=["bcB"])
    p.op("s", lambda b: b.copy(row_bc[:], pall[0:NSTATE, 2048:3072]),
         reads=[cps_], writes=["row_bc"])
    p.dma("sy", lambda b: b.dma_start(out=bcC[:], in_=row_bc[:]),
          reads=["row_bc"], writes=["bcC"])
    for i in range(NH):
        reg(f"Bmb{i}", B0 + 1024 * i, 1024)
        reg(f"Cb{i}", B0 + 8192 + 1024 * i, 1024)
        p.dma("p", lambda b, i=i: b.dma_start(
            out=sb(B0 + 1024 * i, 1024),
            in_=bcB[i:i + 1, :].partition_broadcast(128)),
            reads=["bcB"], writes=[f"Bmb{i}"])
        p.dma("p", lambda b, i=i: b.dma_start(
            out=sb(B0 + 8192 + 1024 * i, 1024),
            in_=bcC[i:i + 1, :].partition_broadcast(128)),
            reads=["bcC"], writes=[f"Cb{i}"])

    for k in range(NT):
        if k % 2 == 0:
            dps, dc0 = regp("ps_dpre", 6, 2), 3072
        else:
            dps, dc0 = regp("ps_dpreB", 2, 2), 1024
        for tc in range(2):
            p.op("te", lambda b, k=k, tc=tc, dc0=dc0: b.matmul(
                pall[:, dc0 + tc * 512:dc0 + (tc + 1) * 512],
                sbbf(OFF_DTPJBF, 1024)[:, 128 * k:128 * (k + 1)],
                sbbf(OFF_DTBF, 512)[:, tc * 512:(tc + 1) * 512],
                start=True, stop=True),
                reads=["dtbf", "dtpjbf"], writes=[dps])
        p.op("v", lambda b, k=k, dc0=dc0: b.tensor_scalar(
            sb(OFF_UACC, 1024), pall[:, dc0:dc0 + 1024],
            arena[:, OFF_DTPB + k:OFF_DTPB + k + 1], 0.0,
            ALU.add, ALU.max),
            reads=[dps, "dtpb"], writes=["uacc"])
        p.op("s", lambda b, k=k, dc0=dc0: b.activation(
            sb(OFF_XN2, 1024), pall[:, dc0:dc0 + 1024], AF.Abs,
            bias=arena[:, OFF_DTPB + k:OFF_DTPB + k + 1]),
            reads=[dps, "dtpb"], writes=["xn2"])
        p.op("s", lambda b: b.activation(
            sb(OFF_XN2, 1024), sb(OFF_XN2, 1024), AF.Exp, scale=cst(C_NEG1)),
            reads=["xn2", "consts"], writes=["xn2"])
        p.op("s", lambda b: b.activation(
            sb(OFF_XN2, 1024), sb(OFF_XN2, 1024), AF.Ln, bias=cst(C_ONE)),
            reads=["xn2", "consts"], writes=["xn2"])
        p.op("v", lambda b: b.tensor_tensor(
            sb(OFF_XN2, 1024), sb(OFF_XN2, 1024), sb(OFF_UACC, 1024), ALU.add),
            reads=["xn2", "uacc"], writes=["xn2"])
        p.dma("sy", lambda b, k=k: b.dma_start(
            out=dsp[128 * k:128 * (k + 1), :], in_=sb(OFF_XN2, 1024)),
            reads=["xn2"], writes=[f"dsp{k}"])

    for r in range(12):
        reg(f"ring{r}", RING[r], 1024)
    for i in range(NH):
        reg(f"Bmb{i}", B0 + 1024 * i, 1024)
        reg(f"Cb{i}", B0 + 8192 + 1024 * i, 1024)
    for par in range(2):
        reg(f"e0p{par}", E0P[par], 1024)
        reg(f"delta{par}", OFF_DELTA2[par], 1024)
        reg(f"ut{par}", OFF_UT2[par], 1024)
    reg("dbu", OFF_DB, 2048)
    reg("du", OFF_DU, 1024)
    reg("yb", OFF_Y, 1024)
    reg("x1l", OFF_X1L, 1024)
    reg("gathA", OFF_GATH, 64)
    reg("gathB", OFF_GATH + 64, 64)
    reg("summA", OFF_SUMM, 8)
    reg("summB", OFF_SUMM + 8, 8)
    reg("hinA", OFF_HIN, 4)
    reg("hinB", OFF_HIN + 4, 4)
    reg("tch", OFF_TCH, 64)
    reg("tsc", OFF_TSC, 64)

    def p3_build_head(H, k):
        par = k % 2
        dlt, dbuf = OFF_DELTA2[par], f"delta{par}"
        utt, ubuf = OFF_UT2[par], f"ut{par}"
        p.dma("sy", lambda b, k=k, dlt=dlt: b.dma_start(
            out=sb(dlt, 1024), in_=dsp[128 * k:128 * (k + 1), :]),
            reads=[f"dsp{k}"], writes=[dbuf])
        p.dma("sy", lambda b, k=k, utt=utt: b.dma_start(
            out=sb(utt, 1024), in_=usp[128 * k:128 * (k + 1), :]),
            reads=[f"usp{k}"], writes=[ubuf])
        p.op("v", lambda b, dlt=dlt, utt=utt: b.tensor_tensor(
            sb(OFF_DU, 1024), sb(dlt, 1024), sb(utt, 1024), ALU.mult),
            reads=[dbuf, ubuf], writes=["du"])
        if H == 0:
            p.op("v", lambda b, dlt=dlt: b.tensor_reduce(
                arena[:, OFF_TCH:OFF_TCH + 1], sb(dlt, 1024),
                mybir.AxisListType.X, ALU.add),
                reads=[dbuf], writes=["tch"])
            p.op("s", lambda b, k=k, H=H: b.activation(
                arena[:, OFF_SUMM + 8:OFF_SUMM + 16],
                arena[:, OFF_ASB + 16 * k + 8 * H:OFF_ASB + 16 * k + 8 * H + 8],
                AF.Exp, scale=arena[:, OFF_TCH:OFF_TCH + 1]),
                reads=["Asb", "tch"], writes=["summB"])

    def p3_build_step(H, k, i):
        par = k % 2
        dlt, dbuf = OFF_DELTA2[par], f"delta{par}"
        r0 = (8 * k + i) % 12
        rs = RING[r0]
        rdeps = [f"ring{r0}", f"ring{r0 + 1}"]
        for q in range(2):
            n = H * NH + i + q
            p.op("s", lambda b, k=k, n=n, q=q, rs=rs, dlt=dlt: b.activation(
                sb(rs + 1024 * q, 1024), sb(dlt, 1024), AF.Exp,
                scale=arena[:, OFF_ASB + 16 * k + n:OFF_ASB + 16 * k + n + 1]),
                reads=[dbuf, "Asb"], writes=[rdeps[q]])
        p.op("v", lambda b, rs=rs: b.memset(arena[:, rs + 1024:rs + 1025], 0.0),
            reads=[], writes=[rdeps[1]])
        p.op("v", lambda b, i=i: b.tensor_tensor(
            sb(OFF_DB, 2048).rearrange("p (w t) -> p w t", w=2),
            sb(OFF_DU, 1024).rearrange("p (w t) -> p w t", w=1)
            .to_broadcast([128, 2, 1024]),
            sb(B0 + 1024 * i, 2048).rearrange("p (w t) -> p w t", w=2),
            ALU.mult),
            reads=["du", f"Bmb{i}", f"Bmb{i + 1}"], writes=["dbu"])
        p.op("v", lambda b, rs=rs: b.tensor_tensor_scan(
            sb(rs, 2048), sb(rs, 2048), sb(OFF_DB, 2048), 0.0,
            ALU.mult, ALU.add),
            reads=rdeps + ["dbu"], writes=rdeps)

    def p3_gather_a(H, k):
        g0 = (8 * k) % 12
        runs = [(g0, 0, 8)] if g0 + 8 <= 12 else [(g0, 0, 12 - g0),
                                                  (0, 12 - g0, 8 - (12 - g0))]
        rdeps = [f"ring{(8 * k + q) % 12}" for q in range(8)]
        for (rstart, c0, ln) in runs:
            p.op("v", lambda b, rstart=rstart, c0=c0, ln=ln: b.tensor_copy(
                arena[:, OFF_SUMM + c0:OFF_SUMM + c0 + ln]
                .rearrange("p (i o) -> p i o", o=1),
                arena[:, A0 + 1024 * rstart:A0 + 1024 * (rstart + ln)]
                .rearrange("p (i t) -> p i t", i=ln)[:, :, 1023:1024]),
                reads=rdeps, writes=["summA"])

    def p3_gather_b(H, k):
        r = H * NT + k
        p.dma("p", lambda b, r=r: b.dma_start(
            out=ccin[r][:], in_=sb(OFF_SUMM, 16)),
            reads=["summA", "summB"], writes=[f"ccin{r}"])
        p.cc(lambda b, r=r: b.collective_compute(
            "AllGather", ALU.bypass, replica_groups=[list(range(NCORES))],
            ins=[ccin[r].opt()], outs=[ccout[r].opt()]),
            reads=[f"ccin{r}"], writes=[f"ccout{r}"])
        p.dma("p", lambda b, r=r: b.dma_start(
            out=sb(OFF_GATH, 128).rearrange("p (j c) -> p j c", j=8),
            in_=ccout[r].rearrange("(j p) c -> p j c", p=128)),
            reads=[f"ccout{r}"], writes=["gathA"])

    def p3_tchain(H, j):
        gv = arena[:, OFF_GATH:OFF_GATH + 128].rearrange(
            "p (j2 h n) -> p j2 h n", j2=8, h=2)
        p.op("v", lambda b, gv=gv: b.tensor_tensor(
            sb(OFF_TCH, 64).rearrange("p (n j2) -> p n j2", n=8),
            gv[:, :, 1, :].transpose([0, 2, 1]),
            sb(OFF_CM8, 64).rearrange("p (n j2) -> p n j2", n=8),
            ALU.mult),
            reads=["gathA", "cm8"], writes=["tch"])
        p.op("v", lambda b, gv=gv: b.tensor_copy(
            sb(OFF_TSC, 64).rearrange("p (n j2) -> p n j2", n=8),
            gv[:, :, 0, :].transpose([0, 2, 1])),
            reads=["gathA"], writes=["tsc"])
        p.op("v", lambda b: b.tensor_tensor_scan(
            sb(OFF_TSC, 64), sb(OFF_TCH, 64), sb(OFF_TSC, 64),
            0.0, ALU.mult, ALU.add),
            reads=["tch", "tsc"], writes=["tsc"])
        p.op("v", lambda b: b.tensor_tensor(
            sb(OFF_TCH, 64), sb(OFF_TSC, 64),
            arena[:, OFF_PREV8:OFF_PREV8 + 64], ALU.mult),
            reads=["tsc", "prev8"], writes=["tch"])
        p.op("v", lambda b: b.tensor_reduce(
            sb(OFF_HIN, 8), sb(OFF_TCH, 64).rearrange("p (n j2) -> p n j2", n=8),
            mybir.AxisListType.X, ALU.add),
            reads=["tch"], writes=["hinA"])

    def p3_post_step(H, j, i):
        par = j % 2
        dlt, dbuf = OFF_DELTA2[par], f"delta{par}"
        r0 = (8 * j + i) % 12
        rs = RING[r0]
        rdeps = [f"ring{r0}", f"ring{r0 + 1}"]
        hname = "hinA"
        for q in range(2):
            n = H * NH + i + q
            e0 = E0P[q]
            p.op("s", lambda b, j=j, n=n, e0=e0, dlt=dlt: b.activation(
                sb(e0, 1024), sb(dlt, 1024), AF.Exp,
                scale=arena[:, OFF_ASB + 16 * j + n:OFF_ASB + 16 * j + n + 1]),
                reads=[dbuf, "Asb"], writes=[f"e0p{q}"])
            p.op("v", lambda b, i=i, q=q, e0=e0, rs=rs: b.scalar_tensor_tensor(
                sb(rs + 1024 * q, 1024), sb(e0, 1024),
                arena[:, OFF_HIN + i + q:OFF_HIN + i + q + 1],
                sb(rs + 1024 * q, 1024), ALU.mult, ALU.add),
                reads=[f"e0p{q}", hname, rdeps[q]], writes=[rdeps[q]])
        p.op("v", lambda b, i=i, rs=rs: b.tensor_tensor(
            sb(rs, 2048), sb(rs, 2048),
            sb(B0 + 8192 + 1024 * i, 2048), ALU.mult),
            reads=rdeps + [f"Cb{i}", f"Cb{i + 1}"], writes=rdeps)
        if H == 0 and i == 0:
            p.op("v", lambda b, rs=rs: b.tensor_tensor(
                sb(OFF_Y, 1024), sb(rs, 1024), sb(rs + 1024, 1024), ALU.add),
                reads=rdeps, writes=["yb"])
        else:
            for q in range(2):
                p.op("v", lambda b, q=q, rs=rs: b.tensor_tensor(
                    sb(OFF_Y, 1024), sb(OFF_Y, 1024),
                    sb(rs + 1024 * q, 1024), ALU.add),
                    reads=["yb", rdeps[q]], writes=["yb"])

    def p3_y_init(H, j):
        if H == 1:
            p.dma("sy", lambda b, j=j: b.dma_start(
                out=sb(OFF_Y, 1024), in_=ysp[128 * j:128 * (j + 1), :]),
                reads=[f"ysp{j}"], writes=["yb"])

    def p3_sdelta(H, k):
        par = k % 2
        dlt, dbuf = OFF_DELTA2[par], f"delta{par}"
        if H == 0:
            p.op("v", lambda b, dlt=dlt: b.tensor_tensor_scan(
                sb(dlt, 1024), sb(OFF_ONESC, 1).to_broadcast([128, 1024]),
                sb(dlt, 1024), 0.0, ALU.mult, ALU.add),
                reads=["onesc", dbuf], writes=[dbuf])
            p.dma("sc", lambda b, k=k, dlt=dlt: b.dma_start(
                out=sdsp[128 * k:128 * (k + 1), :], in_=sb(dlt, 1024)),
                reads=[dbuf], writes=[f"sdsp{k}"])
        else:
            p.dma("sc", lambda b, k=k, dlt=dlt: b.dma_start(
                out=sb(dlt, 1024), in_=sdsp[128 * k:128 * (k + 1), :]),
                reads=[f"sdsp{k}"], writes=[dbuf])

    def p3_d8_late(k):
        dlt, dbuf = OFF_DELTA2[k % 2], f"delta{k % 2}"
        p.op("s", lambda b, k=k, dlt=dlt: b.activation(
            arena[:, OFF_SUMM + 8:OFF_SUMM + 16],
            arena[:, OFF_ASB + 16 * k + 8:OFF_ASB + 16 * k + 16],
            AF.Exp, scale=arena[:, dlt + 1023:dlt + 1024]),
            reads=["Asb", dbuf], writes=["summB"])

    def p3_post_tail(H, j):
        par = j % 2
        utt, ubuf = OFF_UT2[par], f"ut{par}"
        if H == 0:
            p.dma("p", lambda b, j=j: b.dma_start(
                out=ysp[128 * j:128 * (j + 1), :], in_=sb(OFF_Y, 1024)),
                reads=["yb"], writes=[f"ysp{j}"])
        else:
            p.op("v", lambda b, j=j, utt=utt: b.scalar_tensor_tensor(
                sb(OFF_Y, 1024), sb(utt, 1024),
                arena[:, OFF_DSKIP + j:OFF_DSKIP + j + 1], sb(OFF_Y, 1024),
                ALU.mult, ALU.add),
                reads=[ubuf, "dskip", "yb"], writes=["yb"])
            p.dma("sy", lambda b, j=j: b.dma_start(
                out=sb(OFF_X1L, 1024), in_=x1sp[128 * j:128 * (j + 1), :]),
                reads=[f"x1sp{j}"], writes=["x1l"])
            p.op("v", lambda b: b.tensor_tensor(
                sb(OFF_Y, 1024), sb(OFF_Y, 1024), sb(OFF_X1L, 1024), ALU.add),
                reads=["yb", "x1l"], writes=["yb"])
            p.dma("p", lambda b, j=j: b.dma_start(
                out=x2sp[128 * j:128 * (j + 1), :], in_=sb(OFF_Y, 1024)),
                reads=["yb"], writes=[f"x2sp{j}"])
            p.op("s", lambda b: b.activation(
                sbbf(OFF_SDELTA, 512)[:, 0:1024], sb(OFF_Y, 1024), AF.Copy),
                reads=["yb"], writes=["sdelta"])
            p.op("s", lambda b: b.activation(
                sbbf(OFF_SDELTA + 512, 512)[:, 0:1024],
                sbbf(OFF_SDELTA, 512)[:, 0:1024], AF.Square),
                reads=["sdelta"], writes=["sdelta"])
            for tc3 in range(2):
                p.op("te", lambda b, j=j, tc3=tc3: b.matmul(
                    pall[0:1, tc3 * 512:(tc3 + 1) * 512],
                    sbbf(OFF_ONESBF, 1)[:, 0:1],
                    sbbf(OFF_SDELTA, 512)[:, tc3 * 512:(tc3 + 1) * 512],
                    start=(j == 0), stop=(j == NT - 1)),
                    reads=["sdelta", "onesbf"], writes=[regp("ps_sx_ln3", 0, 2)],
                    inc=False)
                p.op("te", lambda b, j=j, tc3=tc3: b.matmul(
                    pall[0:1, 1024 + tc3 * 512:1024 + (tc3 + 1) * 512],
                    sbbf(OFF_ONESBF, 1)[:, 0:1],
                    sbbf(OFF_SDELTA + 512, 512)[:, tc3 * 512:(tc3 + 1) * 512],
                    start=(j == 0), stop=(j == NT - 1)),
                    reads=["sdelta", "onesbf"], writes=[regp("ps_sxx_ln3", 2, 2)],
                    inc=(tc3 == 1))

    def p3_bcast(H):
        for i in range(NH):
            n = H * NH + i
            p.dma("p", lambda b, n=n, i=i: b.dma_start(
                out=sb(B0 + 1024 * i, 1024),
                in_=bcB[n:n + 1, :].partition_broadcast(128)),
                reads=["bcB"], writes=[f"Bmb{i}"])
            p.dma("p", lambda b, n=n, i=i: b.dma_start(
                out=sb(B0 + 8192 + 1024 * i, 1024),
                in_=bcC[n:n + 1, :].partition_broadcast(128)),
                reads=["bcC"], writes=[f"Cb{i}"])

    for H in range(2):
        for ph in range(NT + 1):
            build = ph < NT
            post = ph >= 1
            if build:
                p3_build_head(H, ph)
            if post:
                p3_tchain(H, ph - 1)
                p3_y_init(H, ph - 1)
            if build:
                p3_build_step(H, ph, 0)
            for i in range(0, 6, 2):
                if post:
                    p3_post_step(H, ph - 1, i)
                if build:
                    p3_build_step(H, ph, i + 2)
            if build and H == 1:
                p3_sdelta(H, ph)
                p3_gather_a(H, ph)
                if post:
                    p3_post_step(H, ph - 1, 6)
                p3_d8_late(ph)
                p3_gather_b(H, ph)
            elif build:
                p3_gather_a(H, ph)
                p3_gather_b(H, ph)
                if post:
                    p3_post_step(H, ph - 1, 6)
                p3_sdelta(H, ph)
            elif post:
                p3_post_step(H, ph - 1, 6)
            if H == 0 and ph == NT:
                p3_bcast(1)
            if post:
                p3_post_tail(H, ph - 1)

    for k in range(NT):
        p.dma("sy" if k % 2 == 0 else "sc", lambda b, k=k: b.dma_start(
            out=sbbf(B0 + 1024 * k, 1024)[:, 0:2048],
            in_=wbsp[128 * k:128 * (k + 1), :]),
            reads=[f"wbsp{k}"], writes=[f"wbf{k}"])

    for k in range(NT):
        p.dma("p", lambda b, k=k: b.dma_start(
            out=sb(A0 + 1024 * k, 1024), in_=x2sp[128 * k:128 * (k + 1), :]),
            reads=[f"x2sp{k}"], writes=[f"xres{k}"])
    def p4_out(o, tc, t1off, t1buf):
        p.op("v", lambda b, o=o, tc=tc, t1off=t1off: b.tensor_tensor(
            sb(t1off, 512),
            arena[:, A0 + 1024 * o + 512 * tc:A0 + 1024 * o + 512 * (tc + 1)],
            sb(t1off, 512), ALU.add),
            reads=[t1buf, f"xres{o}"], writes=[t1buf])
        p.dma("p", lambda b, o=o, tc=tc, t1off=t1off: b.dma_start(
            out=out_h[128 * o:128 * (o + 1), 512 * tc:512 * (tc + 1)],
            in_=sb(t1off, 512)),
            reads=[t1buf], writes=["out"])

    for k in range(NT):
        reg(f"xn3bf{k}", C0 + 256 * k, 256)
    emit_ffn(lambda k: A0 + 1024 * k, lambda k: f"xres{k}",
             C0, lambda k: f"xn3bf{k}", p4_out, "ln3", skip_stats=True)

    final = [(s, p.cnt[s])
             for s in ("dsy", "dsy1", "dsy2", "dgp", "dgp1", "dgp2",
                       "dsc", "dsc1", "dsc2")
             if s in p.cnt]
    final += [(s, 1) for s in p.ccnames]
    p.wait_only("sy", final)

    p.emit(nc, stack)
    return nc, stack


_CACHE = {}


def kernel(L0_attn_x, L0_attn_conv_w, L0_attn_conv_b, L0_attn_xproj_w,
           L0_attn_dtproj_w, L0_attn_dtproj_b, L0_attn_A_log, L0_attn_D_skip,
           L0_ffn_W_base, L0_ffn_b_base, L0_ffn_router_W, L0_ffn_A, L0_ffn_Bw):
    if "nc" not in _CACHE:
        nc, stack = build_program()
        _CACHE["nc"] = nc
        _CACHE["stack"] = stack
    nc = _CACHE["nc"]

    f32 = np.float32

    def tilemaj(v):
        C = v.shape[1]
        return np.ascontiguousarray(
            v.reshape(NT, 128, C).transpose(1, 0, 2).reshape(128, NT * C))

    x = np.asarray(L0_attn_x, dtype=f32)
    shared = {
        "wbase_t": np.ascontiguousarray(np.asarray(L0_ffn_W_base, f32).T),
        "router_t": tilemaj(np.asarray(L0_ffn_router_W, f32).T),
        "a2_t": tilemaj(
            np.transpose(np.asarray(L0_ffn_A, f32), (2, 0, 1)).reshape(D, 128)),
        "bw2_t": np.ascontiguousarray(
            np.transpose(np.asarray(L0_ffn_Bw, f32), (0, 2, 1)).reshape(128, D)),
        "convw": tilemaj(np.asarray(L0_attn_conv_w, f32)[:, 0, :]),
        "convb": tilemaj(np.asarray(L0_attn_conv_b, f32)[:, None]),
        "xproj_t": tilemaj(np.asarray(L0_attn_xproj_w, f32).T),
        "dtproj_t": np.ascontiguousarray(np.asarray(L0_attn_dtproj_w, f32).T),
        "dtprojb": tilemaj(np.asarray(L0_attn_dtproj_b, f32)[:, None]),
        "alog": tilemaj(np.asarray(L0_attn_A_log, f32)),
        "dskip": tilemaj(np.asarray(L0_attn_D_skip, f32)[:, None]),
        "bbase": tilemaj(np.asarray(L0_ffn_b_base, f32)[:, None]),
        "repmat": np.ascontiguousarray(
            np.repeat(np.eye(NEXP, dtype=f32) * 2.0, R, axis=1)),
        "ident": np.eye(128, dtype=f32),
        "consts": np.tile(np.array([LNEPS, 1.0, -1.0, -0.5, 0.5], f32), (128, 1)),
    }
    in_maps = []
    for c in range(NCORES):
        bb = c // 4
        t0 = (c % 4) * T
        prevhot = np.zeros((128, NCORES), f32)
        if c % 4 != 0:
            prevhot[:, c - 1] = 1.0
        cmb = np.tile(np.array([0, 1, 1, 1, 0, 1, 1, 1], f32), (128, 1))
        cm8 = np.tile(np.tile(np.array([0, 1, 1, 1, 0, 1, 1, 1], f32), 8), (128, 1))
        prev8 = np.tile(np.tile(prevhot[0], 8), (128, 1)).astype(f32)
        m = dict(shared)
        m["x"] = np.ascontiguousarray(x[bb, t0:t0 + T, :].T)
        m["prevhot"] = prevhot
        m["cmb"] = cmb
        m["cm8"] = cm8
        m["prev8"] = prev8
        m["prev24"] = np.tile(np.repeat(prevhot[0], 3), (128, 1)).astype(f32)
        in_maps.append(m)

    res = run_bass_kernel_spmd(nc, in_maps, core_ids=list(range(NCORES)))
    _CACHE["res"] = res
    out = np.empty((B, S, D), dtype=f32)
    for c in range(NCORES):
        bb = c // 4
        t0 = (c % 4) * T
        out[bb, t0:t0 + T, :] = res.results[c]["out"].T
    return out
